# version 29 (final)
# baseline (speedup 1.0000x reference)
import numpy as np
import concourse.bass as bass
import concourse.mybir as mybir
from concourse.bass_utils import run_bass_kernel_spmd

F32 = mybir.dt.float32
BF16 = mybir.dt.bfloat16
ALU = mybir.AluOpType
ACT = mybir.ActivationFunctionType
AXX = mybir.AxisListType.X

NCORE = 8
T_FULL = 16384
D = 1024
H = 4096
NTOK = T_FULL // NCORE
NT = 256
NTILE = NTOK // NT
PRE = T_FULL - NTOK
PREB = PRE // NT
EPS_RMS = 1e-6
EPS_LN = 1e-5
EPS_TERN = 1e-5
KC = 8
MC = H // 128
MSTRIPE = 256
NSTRIPE = H // MSTRIPE
_DBG = False
_DBG_TILE = 0
_DBG_STALL = True
_RACECHECK = False


class Ct:
    def __init__(self, sem):
        self.sem = sem
        self.n = 0


def build_program():
    from contextlib import ExitStack

    nc = bass.Bass("TRN2", target_bir_lowering=False,
                   detect_race_conditions=_RACECHECK)

    xm = nc.dram_tensor("xm", [NTOK, D], F32, kind="ExternalInput").ap()
    xp = nc.dram_tensor("xp", [PRE, D], F32, kind="ExternalInput").ap()
    xprev = nc.dram_tensor("xprev", [1, D], F32, kind="ExternalInput").ap()
    xnext = nc.dram_tensor("xnext", [1, D], F32, kind="ExternalInput").ap()
    w_gate = nc.dram_tensor("w_gate", [D, H], F32, kind="ExternalInput").ap()
    w_up = nc.dram_tensor("w_up", [D, H], F32, kind="ExternalInput").ap()
    w_down = nc.dram_tensor("w_down", [H, D], F32, kind="ExternalInput").ap()
    w_ffn = nc.dram_tensor("w_ffn", [D, D], F32, kind="ExternalInput").ap()
    cvec = nc.dram_tensor("cvec", [D, 8], F32, kind="ExternalInput").ap()
    bgu = nc.dram_tensor("bgu", [H, 2], F32, kind="ExternalInput").ap()
    mnw_in = nc.dram_tensor("mnw", [1, D], F32, kind="ExternalInput").ap()
    ident_in = nc.dram_tensor("ident", [128, 128], F32, kind="ExternalInput").ap()
    y = nc.dram_tensor("y", [NTOK, D], F32, kind="ExternalOutput").ap()
    dbg = {}
    if _DBG:
        for nm in ("d_A", "d_conv", "d_h", "d_xmid", "d_w1", "d_z", "d_v1",
                   "d_u", "d_v2", "d_carry", "d_zh", "d_hh"):
            dbg[nm] = nc.dram_tensor(nm, [128, 8 * 258], F32,
                                     kind="ExternalOutput").ap()

    gate_sc = nc.dram_tensor("gate_sc", [KC, 2, 128, H], BF16, kind="Internal").ap()
    up_sc = nc.dram_tensor("up_sc", [KC, 2, 128, H], BF16, kind="Internal").ap()
    down_sc = nc.dram_tensor("down_sc", [MC, 2, 128, D], BF16, kind="Internal").ap()
    ffnT_sc = nc.dram_tensor("ffnT_sc", [KC, 128, D], BF16, kind="Internal").ap()

    sched = []
    simlog = []

    def mkwait(eng, ct, val=None):
        v = ct.n if val is None else val
        assert v <= ct.n, "wait beyond planned total"
        if v <= 0:
            return
        simlog.append((eng, "w", id(ct), v))
        sched.append((eng, lambda e, s=ct.sem, v=v: e.wait_ge(s, v)))

    def mkinc(eng, ct, amt, fn):
        ct.n += amt
        simlog.append((eng, "i", id(ct), ct.n))
        sched.append((eng, lambda e, fn=fn, s=ct.sem, a=amt: fn(e).then_inc(s, a)))
        return ct.n

    def mkop(eng, fn):
        sched.append((eng, lambda e, fn=fn: fn(e)))

    ctx = ExitStack()
    sb = lambda n, s, d: ctx.enter_context(nc.sbuf_tensor(n, s, d))
    psal = lambda n: ctx.enter_context(nc.psum_tensor(n, [128, 512], F32))
    mksem = lambda n: ctx.enter_context(nc.semaphore(n))

    sQc = Ct(mksem("sQc"))
    sQt = Ct(mksem("sQt"))
    sQw = [Ct(mksem(f"sQw{i}")) for i in range(2)]
    sQx = [Ct(mksem(f"sQx{i}")) for i in range(2)]
    sQh = Ct(mksem("sQh"))
    sQg = [Ct(mksem(f"sQg{i}")) for i in range(2)]
    sQu = [Ct(mksem(f"sQu{i}")) for i in range(2)]
    sQd = [Ct(mksem(f"sQd{i}")) for i in range(3)]
    sQf = [Ct(mksem(f"sQf{i}")) for i in range(2)]
    sOo = [Ct(mksem(f"sOo{i}")) for i in range(2)]
    sG = Ct(mksem("sG"))
    sT = Ct(mksem("sT"))
    sV = Ct(mksem("sV"))
    sC = Ct(mksem("sC"))

    ident_s = sb("ident_s", [128, 128], F32)
    ones_c16 = sb("ones_c16", [128, 1], BF16)
    ones_row = sb("ones_row", [1, 128], F32)
    mnw_row = sb("mnw_row", [1, D], F32)
    cv = sb("cv", [128, 8, 8], F32)
    bnw = sb("bnw", [128, 8], F32)
    w0c = sb("w0c", [128, 8], F32)
    w1c = sb("w1c", [128, 8], F32)
    w2c = sb("w2c", [128, 8], F32)
    svec = sb("svec", [128, 8], F32)
    bgcol = sb("bgcol", [128, MC], F32)
    bucol = sb("bucol", [128, MC], F32)
    scale_col = sb("scale_col", [128, 1], F32)
    rscale_col = sb("rscale_col", [128, 1], F32)
    tabs = sb("tabs", [128, 8], F32)
    sS = sb("sS", [128, 8, 512], F32)
    ones512 = sb("ones512", [128, 512], F32)
    gcarry = sb("gcarry", [128, 8], F32)
    stx = [sb(f"stx{i}", [128, D], F32) for i in range(2)]
    stxr = [sb(f"stxr{i}", [128, D], F32) for i in range(2)]
    rcol = sb("rcol", [128, 1], F32)
    rcol1 = sb("rcol1", [128, 1], F32)
    A = sb("A", [128, 8, 258], F32)
    X = sb("X", [128, 8, NT], F32)
    M = sb("M", [128, 8, NT], F32)
    V = sb("V", [128, 8, NT], F32)
    W = [sb(f"W{i}", [128, 8, NT], F32) for i in range(2)]
    zh = sb("zh", [128, 8, NT], BF16)
    zl = sb("zl", [128, 8, NT], BF16)
    lnh = sb("lnh", [128, 8, NT], BF16)
    lnl = sb("lnl", [128, 8, NT], BF16)
    uhb = sb("uhb", [128, 8, NT], BF16)
    ulb = sb("ulb", [128, 8, NT], BF16)
    hh = sb("hh", [128, 8, NT], BF16)
    hl = sb("hl", [128, 8, NT], BF16)
    sgb = [sb(f"sgb{i}", [128, NT], F32) for i in range(2)]
    htmp = [sb(f"htmp{i}", [128, NT], F32) for i in range(2)]
    gbuf = [sb(f"gbuf{i}", [128, KC, 2, MSTRIPE], BF16) for i in range(2)]
    ubuf = [sb(f"ubuf{i}", [128, KC, 2, MSTRIPE], BF16) for i in range(2)]
    dbuf = [sb(f"dbuf{i}", [128, 2, D], BF16) for i in range(3)]
    fbuf = [sb(f"fbuf{i}", [128, D], BF16) for i in range(2)]
    outst = [sb(f"outst{i}", [128, D], F32) for i in range(2)]
    hrow = sb("hrow", [1, D], F32)
    hjunk = sb("hjunk", [1, D], F32)
    hsr = sb("hsr", [1, 1], F32)
    st_a = sb("st_a", [1, NT], F32)
    st_b = sb("st_b", [1, NT], F32)
    st_c = sb("st_c", [1, NT], F32)
    eps_rms_c = sb("eps_rms_c", [128, 1], F32)
    eps_ln_c = sb("eps_ln_c", [128, 1], F32)
    negh_c = sb("negh_c", [128, 1], F32)
    posh_c = sb("posh_c", [128, 1], F32)
    dbg_misc = sb("dbg_misc", [128, 16], F32)
    dbg_snap = sb("dbg_snap", [128, 96], F32)
    pb = [psal(f"pb{i}") for i in range(8)]

    sS_f = sS.ap().rearrange("p c t -> p (c t)")
    tern = sS.ap().rearrange("p c t -> p (c t)").rearrange(
        "p (a i) -> p a i", i=D)
    wbufh = [sS.ap()[:, 0:4, :].rearrange("p c t -> p (c t)"),
             sS.ap()[:, 4:8, :].rearrange("p c t -> p (c t)")]
    whi = hh.ap().rearrange("p m t -> p (m t)")
    wlo = hl.ap().rearrange("p m t -> p (m t)")
    fevb = lnh.ap().rearrange("p c t -> p (c t)")[:, 0:512]

    ZH = zh.ap().rearrange("p c t -> p (c t)")
    ZL = zl.ap().rearrange("p c t -> p (c t)")
    LNHF = lnh.ap().rearrange("p c t -> p (c t)")
    LNLF = lnl.ap().rearrange("p c t -> p (c t)")
    UHF = uhb.ap().rearrange("p c t -> p (c t)")
    ULF = ulb.ap().rearrange("p c t -> p (c t)")
    XF = X.ap().rearrange("p c t -> p (c t)")
    MF = M.ap().rearrange("p c t -> p (c t)")
    VF = V.ap().rearrange("p c t -> p (c t)")
    WF = [W[i].ap().rearrange("p c t -> p (c t)") for i in range(2)]


    def snap(i):
        return
        def _s(e, i=i):
            c0 = i * 16
            e.tensor_copy(dbg_snap[:, c0:c0 + 1], scale_col[:])
            e.tensor_copy(dbg_snap[:, c0 + 1:c0 + 2], rscale_col[:])
            e.tensor_copy(dbg_snap[:, c0 + 2:c0 + 10], tabs[:])
            e.tensor_copy(dbg_snap[:, c0 + 10:c0 + 11], negh_c[:])
            e.tensor_copy(dbg_snap[:, c0 + 11:c0 + 12], posh_c[:])
            e.tensor_copy(dbg_snap[:, c0 + 12:c0 + 13], eps_rms_c[:])
            return e.tensor_copy(dbg_snap[:, c0 + 13:c0 + 14], eps_ln_c[:])
        mkinc("vector", sV, 1, _s)

    block = ctx.enter_context(nc.Block())

    mkinc("sync", sQc, 16, lambda e: e.dma_start(out=ident_s[:], in_=ident_in))
    mkinc("sync", sQc, 16, lambda e: e.dma_start(out=mnw_row[:], in_=mnw_in))
    mkinc("sync", sQc, 16, lambda e: e.dma_start(
        out=cv[:], in_=cvec.rearrange("(c p) k -> p c k", p=128)))
    mkinc("sync", sQc, 16, lambda e: e.dma_start(
        out=bgcol[:], in_=bgu.rearrange("(m p) k -> p m k", p=128)[:, :, 0],
        allow_slow_non_contiguous=True))
    mkinc("sync", sQc, 16, lambda e: e.dma_start(
        out=bucol[:], in_=bgu.rearrange("(m p) k -> p m k", p=128)[:, :, 1],
        allow_slow_non_contiguous=True))
    mkwait("vector", sQc)

    def p0_consts(e):
        e.memset(ones_c16[:], 1.0)
        e.memset(ones_row[:], 1.0)
        e.memset(ones512[:], 1.0)
        e.memset(gcarry[:], 0.0)
        e.memset(eps_rms_c[:], EPS_RMS)
        e.memset(eps_ln_c[:], EPS_LN)
        e.memset(negh_c[:], -0.5)
        e.memset(posh_c[:], 0.5)
        nw = cv.ap()[:, :, 0]
        e.tensor_mul(w0c[:], cv.ap()[:, :, 1], nw)
        e.tensor_mul(w1c[:], cv.ap()[:, :, 2], nw)
        e.tensor_mul(w2c[:], cv.ap()[:, :, 3], nw)
        return e.tensor_mul(bnw[:], cv.ap()[:, :, 6], nw)
    mkinc("vector", sV, 1, p0_consts)
    mkwait("scalar", sV)
    mkinc("scalar", sC, 1, lambda e: e.activation(svec[:], cv.ap()[:, :, 5], ACT.Tanh))
    mkwait("vector", sC, sC.n)
    snap(0)

    for h in range(2):
        if h == 1:
            mkwait("sync", sC, sC.n)
        mkinc("sync", sQt, 16, lambda e, h=h: e.dma_start(
            out=tern, in_=w_ffn[h * 512:(h + 1) * 512, :].rearrange(
                "(a p) i -> p a i", p=128)))
        mkwait("scalar", sQt)

        def absp(e, h=h):
            junk = [stx[0][:], stx[1][:], XF[:, 0:1024], XF[:, 1024:2048],
                    VF[:, 0:1024], VF[:, 1024:2048], MF[:, 0:1024],
                    MF[:, 1024:2048]]
            last = None
            for a2 in range(4):
                last = e.activation(junk[h * 4 + a2], tern[:, a2, :], ACT.Abs,
                                    accum_out=tabs[:, h * 4 + a2:h * 4 + a2 + 1])
            return last
        mkinc("scalar", sC, 1, absp)
    mkwait("vector", sC, sC.n)
    mkinc("vector", sV, 1, lambda e: e.tensor_reduce(rcol[:], tabs[:], AXX, ALU.add))
    mkwait("tensor", sV)
    trm = mkinc("tensor", sT, 1, lambda e: e.matmul(
        pb[6][0:1, 0:128], rcol[:], ones512[:, 0:128], start=True, stop=True))
    mkwait("vector", sT, trm)

    def p0_scale(e):
        e.tensor_scalar(st_b[0:1, 0:128], pb[6][0:1, 0:128], 1.0 / (D * D),
                        None, ALU.mult)
        e.drain()
        e.tensor_scalar(st_b[0:1, 0:128], st_b[0:1, 0:128], EPS_TERN,
                        None, ALU.max)
        e.drain()
        e.reciprocal(st_c[0:1, 0:128], st_b[0:1, 0:128])
        e.drain()
        return e.tensor_scalar(st_b[0:1, 0:128], st_b[0:1, 0:128], 0.5,
                               None, ALU.mult)
    mkinc("vector", sV, 1, p0_scale)
    mkwait("tensor", sV)
    bctm = mkinc("tensor", sT, 1, lambda e: (
        e.transpose(pb[7][:, 0:1], st_b[0:1, 0:128], ident_s[0:1, 0:1]),
        e.transpose(pb[7][:, 128:129], st_c[0:1, 0:128], ident_s[0:1, 0:1]))[-1])
    mkwait("vector", sT, bctm)
    mkinc("vector", sV, 1, lambda e: (
        e.tensor_copy(scale_col[:], pb[7][:, 0:1]),
        e.tensor_copy(rscale_col[:], pb[7][:, 128:129]))[-1])
    snap(1)

    fmark = {}
    for h in range(2):
        mkwait("sync", sV, sV.n)
        mkinc("sync", sQt, 16, lambda e, h=h: e.dma_start(
            out=tern, in_=w_ffn[h * 512:(h + 1) * 512, :].rearrange(
                "(a p) i -> p a i", p=128)))
        mkwait("vector", sQt)

        for a2 in range(4):
            uv = mkinc("vector", sV, 1, lambda e, a2=a2: e.tensor_scalar(
                tern[:, a2, :], tern[:, a2, :], rscale_col[:, 0:1], None, ALU.mult))
            mkwait("scalar", sV, uv)

            def signs(e, a2=a2):
                e.activation(stx[0][:], tern[:, a2, :], ACT.Sign,
                             bias=negh_c[:, 0:1])
                return e.activation(stx[1][:], tern[:, a2, :], ACT.Sign,
                                    bias=posh_c[:, 0:1])
            sgm = mkinc("scalar", sC, 1, signs)
            mkwait("vector", sC, sgm)

            def combine(e, a2=a2):
                return e.tensor_add(tern[:, a2, :], stx[0][:], stx[1][:])
            mkinc("vector", sV, 1, combine)
        for ic in range(8):
            mkwait("tensor", sV)

            def ftp(e, h=h, ic=ic):
                last = None
                for a2 in range(4):
                    bank = pb[a2 // 2]
                    last = e.transpose(
                        bank[:, (a2 % 2) * 128:(a2 % 2) * 128 + 128],
                        tern[:, a2, ic * 128:(ic + 1) * 128], ident_s[:])
                return last
            tmk = mkinc("tensor", sT, 1, ftp)
            mkwait("vector", sT, tmk)
            if (h, ic) != (0, 0):
                mkwait("vector", sG, sG.n)

            def fev(e):
                e.tensor_copy(fevb[:, 0:256], pb[0][:, 0:256])
                return e.tensor_copy(fevb[:, 256:512], pb[1][:, 0:256])
            vmk = mkinc("vector", sV, 1, fev)
            mkwait("gpsimd", sV, vmk)
            fmark[(h, ic)] = mkinc("gpsimd", sG, 16, lambda e, h=h, ic=ic: e.dma_start(
                out=ffnT_sc[ic][:, h * 512:(h + 1) * 512], in_=fevb))

    snap(2)
    jobs = []
    for k in range(KC):
        for h in range(2):
            jobs.append((w_gate[k * 128:(k + 1) * 128, h * 2048:(h + 1) * 2048],
                         gate_sc[k, 0][:, h * 2048:(h + 1) * 2048],
                         gate_sc[k, 1][:, h * 2048:(h + 1) * 2048]))
    for k in range(KC):
        for h in range(2):
            jobs.append((w_up[k * 128:(k + 1) * 128, h * 2048:(h + 1) * 2048],
                         up_sc[k, 0][:, h * 2048:(h + 1) * 2048],
                         up_sc[k, 1][:, h * 2048:(h + 1) * 2048]))
    for j2 in range(MC // 2):
        jobs.append((w_down[j2 * 256:(j2 + 1) * 256, :].rearrange(
                        "(a p) d -> p a d", p=128),
                     down_sc[j2 * 2:(j2 + 1) * 2, 0].rearrange("m p d -> p m d"),
                     down_sc[j2 * 2:(j2 + 1) * 2, 1].rearrange("m p d -> p m d")))
    cmark, vmark, gmark = {}, {}, {}

    def emit_cast_job(j):
        src, dhi, dlo = jobs[j]
        buf = WF[j % 2]
        three_d = len(src.shape) == 3
        bufd = buf.rearrange("p (a d) -> p a d", d=D) if three_d else buf
        whid = whi.rearrange("p (a d) -> p a d", d=D) if three_d else whi
        wlod = wlo.rearrange("p (a d) -> p a d", d=D) if three_d else wlo
        if j >= 2:
            mkwait("gpsimd", sC, cmark[j - 2])
            mkwait("gpsimd", sV, vmark[j - 2])
        mkinc("gpsimd", sQw[j % 2], 16, lambda e, src=src, bufd=bufd: e.dma_start(
            out=bufd, in_=src))
        mkwait("scalar", sQw[j % 2])
        if j >= 1:
            mkwait("scalar", sG, gmark[j - 1])
        cmark[j] = mkinc("scalar", sC, 1, lambda e, buf=buf: e.activation(
            whi, buf, ACT.Copy))
        mkwait("vector", sC, cmark[j])
        if j >= 1:
            mkwait("vector", sG, gmark[j - 1])
        vmark[j] = mkinc("vector", sV, 1, lambda e, buf=buf: e.tensor_sub(
            wlo, buf, whi))
        mkwait("gpsimd", sC, cmark[j])
        mkinc("gpsimd", sG, 16, lambda e, dhi=dhi, whid=whid: e.dma_start(
            out=dhi, in_=whid))
        mkwait("gpsimd", sV, vmark[j])
        gmark[j] = mkinc("gpsimd", sG, 16, lambda e, dlo=dlo, wlod=wlod: e.dma_start(
            out=dlo, in_=wlod))

    mkwait("vector", sT, sT.n)
    mkwait("vector", sC, sC.n)
    mkwait("vector", sG, sG.n)

    def sbcast(e):
        last = None
        for c in range(8):
            last = e.tensor_scalar(sS.ap()[:, c, :], ones512[:], svec[:, c:c + 1],
                                   None, ALU.mult)
        return last
    mkinc("vector", sV, 1, sbcast)

    tpmk, sqmk, xrmk = {}, {}, {}
    last_xr = [0]
    rcolp = [rcol, rcol1]

    def emit_group(gi, src_ap):
        buf = stx[gi % 2]
        sr = stxr[gi % 2]
        rc = rcolp[gi % 2]
        if gi >= 2:
            mkwait("sync", sC, sqmk[gi - 2])
            mkwait("sync", sV, xrmk[gi - 2])
            if (gi - 2) in tpmk:
                mkwait("sync", sT, tpmk[gi - 2])
        else:
            mkwait("sync", sC, sC.n)
            mkwait("sync", sV, sV.n)
        mkinc("sync", sQx[gi % 2], 16, lambda e, buf=buf, src_ap=src_ap: e.dma_start(
            out=buf[:], in_=src_ap))
        mkwait("scalar", sQx[gi % 2])
        if gi >= 2:
            mkwait("scalar", sT, tpmk[gi - 2])
            mkwait("scalar", sV, xrmk[gi - 2])
        mkop("scalar", lambda e, buf=buf, sr=sr, rc=rc: e.activation(
            sr[:], buf[:], ACT.Square, accum_out=rc[:]))
        mkop("scalar", lambda e: e.drain())
        mkop("scalar", lambda e, rc=rc: e.activation(rc[:], rc[:], ACT.Ln,
                                              bias=eps_rms_c[:, 0:1], scale=1.0 / D))
        mkop("scalar", lambda e: e.drain())
        sqmk[gi] = mkinc("scalar", sC, 1, lambda e, rc=rc: e.activation(
            rc[:], rc[:], ACT.Exp, scale=-0.5))
        mkwait("vector", sC, sqmk[gi])
        xrmk[gi] = mkinc("vector", sV, 1, lambda e, buf=buf, sr=sr, rc=rc:
                         e.tensor_scalar(sr[:], buf[:], rc[:, 0:1], None, ALU.mult))
        last_xr[0] = xrmk[gi]
        mkwait("tensor", sV, xrmk[gi])

        def tp(e, sr=sr):
            last = None
            for c in range(8):
                bank = pb[c // 4]
                last = e.transpose(bank[:, (c % 4) * 128:(c % 4) * 128 + 128],
                                   sr[:, c * 128:(c + 1) * 128], ident_s[:])
            return last
        tpmk[gi] = mkinc("tensor", sT, 1, tp)

    mkwait("scalar", sT, sT.n)

    ev_s = {}
    for blk in range(PREB):
        for g in range(2):
            gi = blk * 2 + g
            emit_group(gi, xp[gi * 128:(gi + 1) * 128, :])
            mkwait("scalar", sT, tpmk[gi])
            ev_s[gi] = mkinc("scalar", sC, 1, lambda e, g=g: e.activation(
                V.ap()[:, 0:4, g * 128:(g + 1) * 128],
                pb[0][:, 0:512].rearrange("p (c t) -> p c t", c=4), ACT.Copy))
            mkwait("vector", sT, tpmk[gi])

            def pre_ev(e, g=g):
                return e.tensor_copy(V.ap()[:, 4:8, g * 128:(g + 1) * 128],
                                     pb[1][:, 0:512].rearrange("p (c t) -> p c t", c=4))
            mkinc("vector", sV, 1, pre_ev)

        mkwait("vector", sC, ev_s[blk * 2 + 1])

        def pre_scan(e, blk=blk):
            e.drain()
            dst = X if blk % 2 == 0 else M
            prev = M if blk % 2 == 0 else X
            last = None
            for c in range(8):
                init = (gcarry[:, c:c + 1] if blk == 0
                        else prev.ap()[:, c, NT - 1:NT])
                last = e.tensor_tensor_scan(dst.ap()[:, c, :],
                                            sS.ap()[:, c, 0:NT],
                                            V.ap()[:, c, :], init,
                                            ALU.mult, ALU.add)
            return last
        mkinc("vector", sV, 1, pre_scan)
        if blk < len(jobs):
            emit_cast_job(blk)

    def pre_carry_out(e):
        src = X if (PREB - 1) % 2 == 0 else M
        last = None
        for c in range(8):
            last = e.tensor_copy(gcarry[:, c:c + 1], src.ap()[:, c, NT - 1:NT])
        return last
    mkinc("vector", sV, 1, pre_carry_out)

    g_sq, g_xr = {}, {}
    tpA, tpX, evA, evX, hTp = {}, {}, {}, {}, {}
    conv_mk, scan_mk, mid_mk = {}, {}, {}
    sqz_mk, stats_mk, rv_mk, r2_mk = {}, {}, {}, {}
    mb_mk, zm_mk, z_s, z_v = {}, {}, {}, {}
    v1_mk, v2_mk = {}, {}
    ln1_sum, ln1_mrow, ln1_mbc, ln1_ctr, ln1_sq, ln1_ssq, ln1_qbc, ln1_ap = \
        {}, {}, {}, {}, {}, {}, {}, {}
    ln2_sum, ln2_mrow, ln2_mbc, ln2_ctr, ln2_sq, ln2_ssq, ln2_qbc, ln2_ap, \
        ln2_rs = {}, {}, {}, {}, {}, {}, {}, {}, {}
    u_s, u_v = {}, {}
    otm, oev, out_omk = {}, {}, {}
    silu_mk, htmp_mk, hhc_mk, hls_mk = {}, {}, {}, {}
    halo_sq, halo_xr = {}, {}
    hstate = {"id": -1}
    rhalo_ev = {}
    gu_mmk, dn_mmk, ff_mmk = {}, {}, {}

    def kwarm():
        mkop("tensor", lambda e: e.matmul(pb[4][0:1, 0:128], ones512[:, 0:1],
                                          ident_s[:, 0:128],
                                          start=True, stop=True))

    def w_load(t, g):
        mkwait("sync", sC, g_sq.get((t - 1, g), 0))
        mkwait("sync", sV, g_xr.get((t - 1, g), 0))
        mkwait("sync", sT, tpX.get((t - 1, g), 0))
        src = xm[(t * NT + g * 128):(t * NT + (g + 1) * 128), :]
        mkinc("sync", sQx[g], 16, lambda e, src=src, g=g: e.dma_start(
            out=stx[g][:], in_=src))

    def w_rms(t, g):
        buf, sr = stx[g], stxr[g]
        mkwait("scalar", sQx[g])
        mkwait("scalar", sT, tpA.get((t - 1, g), 0))
        mkwait("scalar", sV, last_xr[0])
        mkop("scalar", lambda e: e.activation(sr[:], buf[:], ACT.Square,
                                              accum_out=rcol[:]))
        mkop("scalar", lambda e: e.drain())
        mkop("scalar", lambda e: e.activation(rcol[:], rcol[:], ACT.Ln,
                                              bias=eps_rms_c[:, 0:1], scale=1.0 / D))
        mkop("scalar", lambda e: e.drain())
        g_sq[(t, g)] = mkinc("scalar", sC, 1, lambda e: e.activation(
            rcol[:], rcol[:], ACT.Exp, scale=-0.5))

    def w_xr(t, g):
        buf, sr = stx[g], stxr[g]
        mkwait("vector", sC, g_sq[(t, g)])
        g_xr[(t, g)] = mkinc("vector", sV, 1, lambda e: e.tensor_scalar(
            sr[:], buf[:], rcol[:, 0:1], None, ALU.mult))
        last_xr[0] = g_xr[(t, g)]

    def w_halo_S(src_ap):
        hid = hstate["id"]
        mkwait("sync", sC, halo_sq.get(hid, 0))
        mkwait("sync", sV, halo_xr.get(hid, 0))
        mkwait("sync", sT, hTp.get(hid, 0))
        hstate["id"] = hid = hid + 1
        mkinc("sync", sQh, 16, lambda e: e.dma_start(out=hrow[:], in_=src_ap))
        mkwait("scalar", sQh)
        mkop("scalar", lambda e: e.activation(hjunk[:], hrow[:], ACT.Square,
                                              accum_out=hsr[:]))
        mkop("scalar", lambda e: e.drain())
        mkop("scalar", lambda e: e.activation(hsr[:], hsr[:], ACT.Ln,
                                              bias=eps_rms_c[0:1, 0:1],
                                              scale=1.0 / D))
        mkop("scalar", lambda e: e.drain())
        halo_sq[hid] = mkinc("scalar", sC, 1, lambda e: e.activation(
            hsr[:], hsr[:], ACT.Exp, scale=-0.5))

    def w_halo_xr():
        hid = hstate["id"]
        mkwait("vector", sC, halo_sq[hid])
        halo_xr[hid] = mkinc("vector", sV, 1, lambda e: e.tensor_scalar(
            hrow[:], hrow[:], hsr[0:1, 0:1], None, ALU.mult))

    def w_halo_tp_ev(dst_col, war_sv):
        hid = hstate["id"]
        mkwait("tensor", sV, halo_xr[hid])
        mkwait("tensor", sV, war_sv)

        def htp(e):
            last = None
            for c in range(8):
                last = e.transpose(pb[6][:, c:c + 1],
                                   hrow[0:1, c * 128:(c + 1) * 128],
                                   ident_s[0:1, 0:1])
            return last
        hTp[hid] = mkinc("tensor", sT, 1, htp)
        mkwait("vector", sT, hTp[hid])
        ev = mkinc("vector", sV, 1, lambda e, dst_col=dst_col: e.tensor_copy(
            A.ap()[:, :, dst_col:dst_col + 1].rearrange("p c t -> p (c t)"),
            pb[6][:, 0:8]))
        mkop("vector", lambda e: e.drain())
        return ev

    def w_lhcopy(t):
        mkop("vector", lambda e: e.tensor_copy(
            A.ap()[:, :, 0:1], A.ap()[:, :, 256:257]))

    def w_tpA(t, g, war_sv):
        mkwait("tensor", sV, g_xr[(t, g)])
        mkwait("tensor", sV, war_sv)

        def tp(e, g=g):
            last = None
            for c in range(8):
                bank = pb[6 + c // 4]
                last = e.transpose(bank[:, (c % 4) * 128:(c % 4) * 128 + 128],
                                   stxr[g][:, c * 128:(c + 1) * 128], ident_s[:])
            return last
        tpA[(t, g)] = mkinc("tensor", sT, 1, tp)

    def w_evA(t, g):
        mkwait("vector", sT, tpA[(t, g)])

        def ev(e, g=g):
            e.tensor_copy(A.ap()[:, 0:4, 1 + g * 128:1 + (g + 1) * 128],
                          pb[6][:, 0:512].rearrange("p (c t) -> p c t", c=4))
            return e.tensor_copy(A.ap()[:, 4:8, 1 + g * 128:1 + (g + 1) * 128],
                                 pb[7][:, 0:512].rearrange("p (c t) -> p c t", c=4))
        evA[(t, g)] = mkinc("vector", sV, 1, ev)

    def w_tpX(t, g, war_sv):
        mkwait("tensor", sV, war_sv)

        def tp(e, g=g):
            last = None
            for c in range(8):
                bank = pb[6 + c // 4]
                last = e.transpose(bank[:, (c % 4) * 128:(c % 4) * 128 + 128],
                                   stx[g][:, c * 128:(c + 1) * 128], ident_s[:])
            return last
        tpX[(t, g)] = mkinc("tensor", sT, 1, tp)

    def w_evX(t, g):
        mkwait("vector", sT, tpX[(t, g)])

        def ev(e, g=g):
            e.tensor_copy(X.ap()[:, 0:4, g * 128:(g + 1) * 128],
                          pb[6][:, 0:512].rearrange("p (c t) -> p c t", c=4))
            return e.tensor_copy(X.ap()[:, 4:8, g * 128:(g + 1) * 128],
                                 pb[7][:, 0:512].rearrange("p (c t) -> p c t", c=4))
        evX[(t, g)] = mkinc("vector", sV, 1, ev)

    def w_conv(t):
        mkwait("vector", sT, otm.get((t - 2, 1), 0))

        def conv_chunk(e):
            last = None
            for c in range(8):
                e.tensor_scalar(M.ap()[:, c, :], A.ap()[:, c, 2:258],
                                w2c[:, c:c + 1], cv.ap()[:, c, 4:5],
                                ALU.mult, ALU.add)
            for c in range(8):
                e.scalar_tensor_tensor(M.ap()[:, c, :], A.ap()[:, c, 0:256],
                                       w0c[:, c:c + 1], M.ap()[:, c, :],
                                       ALU.mult, ALU.add)
            for c in range(8):
                last = e.scalar_tensor_tensor(M.ap()[:, c, :], A.ap()[:, c, 1:257],
                                              w1c[:, c:c + 1], M.ap()[:, c, :],
                                              ALU.mult, ALU.add)
            return last
        conv_mk[t] = mkinc("vector", sV, 1, conv_chunk)

    def w_scan(t):
        mkwait("vector", sT, ln2_sum.get(t - 2, 0))

        def tile_scan(e):
            for c in range(8):
                e.tensor_tensor_scan(V.ap()[:, c, :], sS.ap()[:, c, 0:NT],
                                     A.ap()[:, c, 1:257], gcarry[:, c:c + 1],
                                     ALU.mult, ALU.add)
            last = None
            for c in range(8):
                last = e.tensor_copy(gcarry[:, c:c + 1], V.ap()[:, c, NT - 1:NT])
            return last
        scan_mk[t] = mkinc("vector", sV, 1, tile_scan)

    def w_mid(t):
        def mid_ops(e):
            last = None
            for c in range(8):
                e.scalar_tensor_tensor(M.ap()[:, c, :], V.ap()[:, c, :],
                                       bnw[:, c:c + 1], M.ap()[:, c, :],
                                       ALU.mult, ALU.add)
            for c in range(8):
                e.tensor_add(M.ap()[:, c, :], M.ap()[:, c, :], X.ap()[:, c, :])
            for c in range(8):
                last = e.scalar_tensor_tensor(W[t % 2].ap()[:, c, :],
                                              X.ap()[:, c, :], cv.ap()[:, c, 7:8],
                                              M.ap()[:, c, :], ALU.add, ALU.add)
            return last
        mid_mk[t] = mkinc("vector", sV, 1, mid_ops)

    def w_rms2(t):
        mkwait("scalar", sV, mid_mk[t])
        mkwait("scalar", sT, stats_mk.get(t - 1, 0))
        sqz_mk[t] = mkinc("scalar", sC, 1, lambda e: e.activation(
            LNHF, MF, ACT.Square))
        mkwait("tensor", sC, sqz_mk[t])
        mkwait("tensor", sV, rhalo_ev[t])

        def rms2_stats(e):
            last = None
            for c in range(8):
                last = e.matmul(pb[6][0:1, 0:NT], ones_c16[:], lnh.ap()[:, c, :],
                                start=(c == 0), stop=(c == 7))
            return last
        stats_mk[t] = mkinc("tensor", sT, 1, rms2_stats)
        mkwait("vector", sT, stats_mk[t])
        rv_mk[t] = mkinc("vector", sV, 1, lambda e: e.tensor_scalar(
            st_a[:], pb[6][0:1, 0:NT], 1.0 / D, None, ALU.mult))
        mkwait("scalar", sV, rv_mk[t])
        mkop("scalar", lambda e: e.activation(st_a[:], st_a[:], ACT.Ln,
                                              bias=eps_rms_c[0:1, 0:1]))
        mkop("scalar", lambda e: e.drain())
        r2_mk[t] = mkinc("scalar", sC, 1, lambda e: e.activation(
            st_a[:], st_a[:], ACT.Exp, scale=-0.5))

    def w_mbz(t, wave):
        if wave == 0:
            mkwait("tensor", sC, r2_mk[t])
            mkwait("tensor", sV, rv_mk[t])
        else:
            mkwait("tensor", sV, zm_mk[(t, 0)])

        def mb(e, wave=wave):
            last = None
            for i in range(4):
                c = wave * 4 + i
                bank = pb[6 + i // 2]
                o0 = (i % 2) * NT
                last = e.matmul(bank[:, o0:o0 + NT],
                                mnw_row[0:1, c * 128:(c + 1) * 128], st_a[:],
                                start=(i % 2 == 0), stop=(i % 2 == 1))
            return last
        mb_mk[(t, wave)] = mkinc("tensor", sT, 1, mb)
        mkwait("vector", sT, mb_mk[(t, wave)])

        def zmul(e, wave=wave):
            last = None
            for i in range(4):
                c = wave * 4 + i
                bank = pb[6 + i // 2]
                o0 = (i % 2) * NT
                last = e.tensor_mul(M.ap()[:, c, :], M.ap()[:, c, :],
                                    bank[:, o0:o0 + NT])
            return last
        zm_mk[(t, wave)] = mkinc("vector", sV, 1, zmul)

    def p2_zcast(t):
        mkwait("scalar", sV, zm_mk[(t, 1)])
        if t > 0:
            mkwait("scalar", sT, gu_mmk[(t - 1, NSTRIPE - 1)])
        z_s[t] = mkinc("scalar", sC, 1, lambda e: e.activation(ZH, MF, ACT.Copy))
        mkwait("vector", sC, z_s[t])
        z_v[t] = mkinc("vector", sV, 1, lambda e: e.tensor_sub(ZL, MF, ZH))

    def p2_v1(t):
        mkwait("vector", sT, dn_mmk[(t, MC - 1)])

        def v1_ops(e):
            last = None
            for dc in range(8):
                last = e.tensor_add(W[t % 2].ap()[:, dc, :],
                                    pb[dc // 2][:, (dc % 2) * NT:(dc % 2 + 1) * NT],
                                    W[t % 2].ap()[:, dc, :])
            return last
        v1_mk[t] = mkinc("vector", sV, 1, v1_ops)

    def p2_ln1(t):
        wb = W[t % 2]
        mkwait("tensor", sV, v1_mk[t])

        def l1sum(e):
            last = None
            for c in range(8):
                last = e.matmul(pb[0][0:1, 0:NT], ones512[:, 0:1],
                                wb.ap()[:, c, :], start=(c == 0), stop=(c == 7))
            return last
        ln1_sum[t] = mkinc("tensor", sT, 1, l1sum)
        mkwait("vector", sT, ln1_sum[t])
        ln1_mrow[t] = mkinc("vector", sV, 1, lambda e: e.tensor_scalar(
            st_b[:], pb[0][0:1, 0:NT], 1.0 / D, None, ALU.mult))
        mkwait("tensor", sV, ln1_mrow[t])
        ln1_mbc[t] = mkinc("tensor", sT, 1, lambda e: e.matmul(
            pb[2][:, 0:NT], ones_row[:], st_b[:], start=True, stop=True))
        mkwait("vector", sT, ln1_mbc[t])

        def l1ctr(e):
            last = None
            for c in range(8):
                last = e.tensor_sub(wb.ap()[:, c, :], wb.ap()[:, c, :],
                                    pb[2][:, 0:NT])
            return last
        ln1_ctr[t] = mkinc("vector", sV, 1, l1ctr)
        mkwait("tensor", sV, ln1_ctr[t])
        kwarm()
        mkwait("scalar", sV, ln1_ctr[t])
        mkwait("scalar", sT, ln2_ssq.get(t - 1, 0))
        ln1_sq[t] = mkinc("scalar", sC, 1, lambda e: e.activation(
            LNLF, WF[t % 2], ACT.Square))
        mkwait("tensor", sC, ln1_sq[t])

        def l1ssq(e):
            last = None
            for c in range(8):
                last = e.matmul(pb[1][0:1, 0:NT], ones_c16[:], lnl.ap()[:, c, :],
                                start=(c == 0), stop=(c == 7))
            return last
        ln1_ssq[t] = mkinc("tensor", sT, 1, l1ssq)
        mkwait("vector", sT, ln1_ssq[t])
        vr = mkinc("vector", sV, 1, lambda e: e.tensor_scalar(
            st_c[:], pb[1][0:1, 0:NT], 1.0 / D, None, ALU.mult))
        mkwait("scalar", sV, vr)
        mkop("scalar", lambda e: e.activation(st_c[:], st_c[:], ACT.Ln,
                                              bias=eps_ln_c[0:1, 0:1]))
        mkop("scalar", lambda e: e.drain())
        qk = mkinc("scalar", sC, 1, lambda e: e.activation(
            st_c[:], st_c[:], ACT.Exp, scale=-0.5))
        mkwait("tensor", sC, qk)
        ln1_qbc[t] = mkinc("tensor", sT, 1, lambda e: e.matmul(
            pb[3][:, 0:NT], ones_row[:], st_c[:], start=True, stop=True))
        mkwait("vector", sT, ln1_qbc[t])

        def l1ap(e):
            last = None
            for c in range(8):
                last = e.tensor_mul(wb.ap()[:, c, :], wb.ap()[:, c, :],
                                    pb[3][:, 0:NT])
            return last
        ln1_ap[t] = mkinc("vector", sV, 1, l1ap)

    def p2_ucast(t):
        mkwait("tensor", sV, ln1_ap[t])
        kwarm()
        mkwait("scalar", sV, ln1_ap[t])
        if t > 0:
            mkwait("scalar", sT, ff_mmk[(t - 1, KC - 1)])
        u_s[t] = mkinc("scalar", sC, 1, lambda e: e.activation(
            UHF, WF[t % 2], ACT.Copy))
        mkwait("vector", sC, u_s[t])
        u_v[t] = mkinc("vector", sV, 1, lambda e: e.tensor_sub(
            ULF, WF[t % 2], UHF))

    def p2_ffn(t):
        for ic in range(KC):
            fpf = ic % 2
            if ic >= 2:
                mkwait("gpsimd", sT, ff_mmk[(t, ic - 2)])
            elif t > 0:
                mkwait("gpsimd", sT, ff_mmk[(t - 1, KC - 2 + ic)])
            mkinc("gpsimd", sQf[fpf], 16, lambda e, ic=ic, fpf=fpf: e.dma_start(
                out=fbuf[fpf][:], in_=ffnT_sc[ic]))
            mkwait("tensor", sQf[fpf])
            if ic == 0:
                mkwait("tensor", sV, u_v[t])
                mkwait("tensor", sC, u_s[t])

            def ffn_mm(e, ic=ic, fpf=fpf):
                last = None
                for oc in range(8):
                    ft = fbuf[fpf].ap()[:, oc * 128:(oc + 1) * 128]
                    bk = pb[oc // 2]
                    o0 = (oc % 2) * NT
                    e.matmul(bk[:, o0:o0 + NT], ft, uhb.ap()[:, ic, :],
                             start=(ic == 0 and oc % 2 == 0), stop=False)
                    last = e.matmul(bk[:, o0:o0 + NT], ft, ulb.ap()[:, ic, :],
                                    start=False,
                                    stop=(ic == KC - 1 and oc % 2 == 1))
                return last
            ff_mmk[(t, ic)] = mkinc("tensor", sT, 1, ffn_mm)

    def p2_v2(t):
        mkwait("vector", sT, ff_mmk[(t, KC - 1)])

        def v2_ops(e):
            last = None
            for oc in range(8):
                last = e.scalar_tensor_tensor(
                    V.ap()[:, oc, :],
                    pb[oc // 2][:, (oc % 2) * NT:(oc % 2 + 1) * NT],
                    scale_col[:, 0:1], W[t % 2].ap()[:, oc, :],
                    ALU.mult, ALU.add)
            return last
        v2_mk[t] = mkinc("vector", sV, 1, v2_ops)

    def wv_ln2_a(t):
        mkwait("tensor", sV, v2_mk[t])

        def l2sum(e):
            last = None
            for c in range(8):
                last = e.matmul(pb[6][0:1, 0:NT], ones512[:, 0:1],
                                V.ap()[:, c, :], start=(c == 0), stop=(c == 7))
            return last
        ln2_sum[t] = mkinc("tensor", sT, 1, l2sum)
        mkwait("vector", sT, ln2_sum[t])
        ln2_mrow[t] = mkinc("vector", sV, 1, lambda e: e.tensor_scalar(
            st_b[:], pb[6][0:1, 0:NT], 1.0 / D, None, ALU.mult))

    def wv_ln2_b(t):
        mkwait("tensor", sV, ln2_mrow[t])
        ln2_mbc[t] = mkinc("tensor", sT, 1, lambda e: e.matmul(
            pb[7][:, 0:NT], ones_row[:], st_b[:], start=True, stop=True))
        mkwait("vector", sT, ln2_mbc[t])
        mkwait("vector", sC, z_s.get(t + 1, 0))

        def l2ctr(e):
            last = None
            for c in range(8):
                last = e.tensor_sub(M.ap()[:, c, :], V.ap()[:, c, :],
                                    pb[7][:, 0:NT])
            return last
        ln2_ctr[t] = mkinc("vector", sV, 1, l2ctr)

    def wv_ln2_c(t):
        mkwait("scalar", sV, ln2_ctr[t])
        mkwait("scalar", sT, ln1_ssq[t])
        ln2_sq[t] = mkinc("scalar", sC, 1, lambda e: e.activation(
            LNLF, MF, ACT.Square))
        mkwait("tensor", sC, ln2_sq[t])
        mkwait("tensor", sV, ln2_mrow[t])

        def l2ssq(e):
            last = None
            for c in range(8):
                last = e.matmul(pb[6][0:1, 0:NT], ones_c16[:], lnl.ap()[:, c, :],
                                start=(c == 0), stop=(c == 7))
            return last
        ln2_ssq[t] = mkinc("tensor", sT, 1, l2ssq)
        mkwait("vector", sT, ln2_ssq[t])
        vr = mkinc("vector", sV, 1, lambda e: e.tensor_scalar(
            st_c[:], pb[6][0:1, 0:NT], 1.0 / D, None, ALU.mult))
        mkwait("scalar", sV, vr)
        mkop("scalar", lambda e: e.activation(st_c[:], st_c[:], ACT.Ln,
                                              bias=eps_ln_c[0:1, 0:1]))
        mkop("scalar", lambda e: e.drain())
        ln2_rs[t] = mkinc("scalar", sC, 1, lambda e: e.activation(
            st_c[:], st_c[:], ACT.Exp, scale=-0.5))

    def wv_ln2_d(t):
        mkwait("tensor", sC, ln2_rs[t])
        mkwait("tensor", sV, ln2_ctr[t])
        ln2_qbc[t] = mkinc("tensor", sT, 1, lambda e: e.matmul(
            pb[7][:, 0:NT], ones_row[:], st_c[:], start=True, stop=True))
        mkwait("vector", sT, ln2_qbc[t])

        def l2ap(e):
            last = None
            for c in range(8):
                last = e.tensor_mul(M.ap()[:, c, :], M.ap()[:, c, :],
                                    pb[7][:, 0:NT])
            return last
        ln2_ap[t] = mkinc("vector", sV, 1, l2ap)

    def wv_out(t, g):
        mkwait("tensor", sV, ln2_ap[t] if g == 0 else oev[(t, 0)])

        def out_tp(e, g=g):
            last = None
            for c in range(8):
                bank = pb[6 + c // 4]
                last = e.transpose(bank[:, (c % 4) * 128:(c % 4) * 128 + 128],
                                   M.ap()[:, c, g * 128:(g + 1) * 128],
                                   ident_s[:])
            return last
        otm[(t, g)] = mkinc("tensor", sT, 1, out_tp)
        mkwait("vector", sT, otm[(t, g)])
        mkwait("vector", sOo[g], out_omk.get((t - 1, g), 0))

        def out_ev(e, g=g):
            e.tensor_copy(outst[g][:, 0:512], pb[6][:, 0:512])
            return e.tensor_copy(outst[g][:, 512:1024], pb[7][:, 0:512])
        oev[(t, g)] = mkinc("vector", sV, 1, out_ev)
        mkwait("gpsimd", sV, oev[(t, g)])
        out_omk[(t, g)] = mkinc("gpsimd", sOo[g], 16,
                                lambda e, t=t, g=g: e.dma_start(
                                    out=y[t * NT + g * 128:t * NT + (g + 1) * 128, :],
                                    in_=outst[g][:]))

    def halo_src(t):
        return xnext if t == NTILE - 1 else xm[(t + 1) * NT:(t + 1) * NT + 1, :]

    def emit_dn(ti, mcd):
        dpfd = mcd % 3
        mkwait("tensor", sQd[dpfd])
        mkwait("tensor", sC, hhc_mk[(ti, mcd)])
        mkwait("tensor", sV, hls_mk[(ti, mcd)])
        if mcd == 0:
            mkwait("tensor", sV, v2_mk.get(ti - 1, 0))

        def dn_mm(e, mcd=mcd, dpfd=dpfd):
            last = None
            for dc in range(8):
                wh_ = dbuf[dpfd].ap()[:, 0, dc * 128:(dc + 1) * 128]
                wl_ = dbuf[dpfd].ap()[:, 1, dc * 128:(dc + 1) * 128]
                hch = hh.ap()[:, mcd % 8, :]
                hcl = hl.ap()[:, mcd % 8, :]
                bk = pb[dc // 2]
                o0 = (dc % 2) * NT
                e.matmul(bk[:, o0:o0 + NT], wh_, hch,
                         start=(mcd == 0 and dc % 2 == 0), stop=False)
                e.matmul(bk[:, o0:o0 + NT], wh_, hcl, start=False, stop=False)
                last = e.matmul(bk[:, o0:o0 + NT], wl_, hch, start=False,
                                stop=(mcd == MC - 1 and dc % 2 == 1))
            return last
        dn_mmk[(ti, mcd)] = mkinc("tensor", sT, 1, dn_mm)

    def dbg_tap3(name, src_ap, width):
        if not _DBG:
            return
        mkwait("sync", sV, sV.n)
        mkwait("sync", sC, sC.n)
        mkwait("sync", sT, sT.n)
        mkinc("sync", sOo[0], 16, lambda e, name=name, src_ap=src_ap,
              width=width: e.dma_start(out=dbg[name][:, 0:width], in_=src_ap))
        mkwait("vector", sOo[0], sOo[0].n)
        mkwait("scalar", sOo[0], sOo[0].n)
        mkwait("tensor", sOo[0], sOo[0].n)

    mkwait("sync", sG, sG.n)
    mkwait("vector", sG, sG.n)
    mkwait("scalar", sG, sG.n)
    mkwait("tensor", sV, sV.n)
    mkwait("scalar", sT, sT.n)
    mkwait("vector", sC, sC.n)
    mkwait("sync", sC, sC.n)
    mkwait("sync", sV, sV.n)
    mkwait("sync", sT, sT.n)

    def issue_stripe(ti, si):
        pbf = si % 2
        m0 = si * MSTRIPE
        mkinc("sync", sQg[pbf], 16, lambda e, m0=m0, pbf=pbf: e.dma_start(
            out=gbuf[pbf][:],
            in_=gate_sc.rearrange("k h p m -> p k h m")[:, :, :, m0:m0 + MSTRIPE]))
        mkinc("sync", sQu[pbf], 16, lambda e, m0=m0, pbf=pbf: e.dma_start(
            out=ubuf[pbf][:],
            in_=up_sc.rearrange("k h p m -> p k h m")[:, :, :, m0:m0 + MSTRIPE]))

    w_load(0, 0)
    w_load(0, 1)
    issue_stripe(0, 0)
    issue_stripe(0, 1)
    w_halo_S(xprev)
    w_halo_xr()
    lh_ev = w_halo_tp_ev(0, 0)
    w_rms(0, 0)
    w_xr(0, 0)
    w_rms(0, 1)
    w_xr(0, 1)
    w_tpA(0, 0, lh_ev)
    w_evA(0, 0)
    w_tpX(0, 0, evA[(0, 0)])
    w_evX(0, 0)
    w_tpA(0, 1, evX[(0, 0)])
    w_evA(0, 1)
    w_tpX(0, 1, evA[(0, 1)])
    w_evX(0, 1)
    w_halo_S(halo_src(0))
    w_halo_xr()
    rhalo_ev[0] = w_halo_tp_ev(257, evX[(0, 1)])
    w_conv(0)
    dbg_tap3("d_carry", gcarry[:], 8)
    w_scan(0)
    dbg_tap3("d_conv", VF, 2048)
    w_mid(0)
    w_rms2(0)
    w_mbz(0, 0)
    w_mbz(0, 1)
    p2_zcast(0)

    def dbg_tap2(name, src_ap, width):
        if not _DBG:
            return
        mkwait("sync", sV, sV.n)
        mkwait("sync", sC, sC.n)
        mkwait("sync", sT, sT.n)
        mkinc("sync", sOo[0], 16, lambda e, name=name, src_ap=src_ap,
              width=width: e.dma_start(out=dbg[name][:, 0:width], in_=src_ap))
        mkwait("vector", sOo[0], sOo[0].n)
        mkwait("scalar", sOo[0], sOo[0].n)
        mkwait("tensor", sOo[0], sOo[0].n)

    dbg_tap2("d_A", A.ap().rearrange("p c t -> p (c t)"), 8 * 258)
    dbg_tap2("d_h", VF, 2048)
    dbg_tap2("d_w1", WF[0], 2048)
    dbg_tap2("d_z", MF, 2048)


    for ti in range(NTILE):
        weave = {}

        def addw(si, fn):
            weave.setdefault(si, []).append(fn)
        if ti >= 1:
            tp_ = ti - 1
            addw(1, lambda tp_=tp_: wv_ln2_a(tp_))
            addw(2, lambda tp_=tp_: wv_ln2_b(tp_))
            addw(3, lambda tp_=tp_: wv_ln2_c(tp_))
            addw(4, lambda tp_=tp_: wv_ln2_d(tp_))
            addw(5, lambda tp_=tp_: wv_out(tp_, 0))
            addw(6, lambda tp_=tp_: wv_out(tp_, 1))
        if ti + 1 < NTILE:
            tf = ti + 1
            addw(3, lambda tf=tf: w_load(tf, 0))
            addw(4, lambda tf=tf: (w_rms(tf, 0), w_lhcopy(tf)))
            addw(5, lambda tf=tf: (w_xr(tf, 0), w_load(tf, 1)))
            addw(6, lambda tf=tf: w_rms(tf, 1))
            addw(7, lambda tf=tf: (w_xr(tf, 1),
                                   w_tpA(tf, 0, oev.get((tf - 2, 1), 0)),
                                   w_evA(tf, 0)))
            addw(8, lambda tf=tf: (w_tpX(tf, 0, evA[(tf, 0)]), w_evX(tf, 0),
                                   w_halo_S(halo_src(tf))))
            addw(9, lambda tf=tf: (w_tpA(tf, 1, evX[(tf, 0)]), w_evA(tf, 1),
                                   w_halo_xr()))
            addw(10, lambda tf=tf: (w_tpX(tf, 1, evA[(tf, 1)]), w_evX(tf, 1)))
            addw(11, lambda tf=tf: (rhalo_ev.__setitem__(
                                        tf, w_halo_tp_ev(257, evX[(tf, 1)])),
                                    w_conv(tf)))
            addw(12, lambda tf=tf: (w_scan(tf), w_mid(tf)))
            addw(13, lambda tf=tf: w_rms2(tf))
            addw(14, lambda tf=tf: (w_mbz(tf, 0), w_mbz(tf, 1)))

        for si in range(NSTRIPE):
            for fn in weave.get(si, []):
                fn()
            pbf = si % 2
            if not (ti == 0 and si < 2):
                if si >= 2:
                    mkwait("sync", sT, gu_mmk[(ti, si - 2)])
                elif ti > 0:
                    mkwait("sync", sT, gu_mmk[(ti - 1, NSTRIPE - 2 + si)])
                issue_stripe(ti, si)
            if si == 0:
                mkwait("tensor", sC, z_s[ti])
                mkwait("tensor", sV, z_v[ti])
            mkwait("tensor", sQg[pbf])
            mkwait("tensor", sQu[pbf])
            for sub in range(MSTRIPE // 128):
                mc = si * (MSTRIPE // 128) + sub
                bank = pb[4 + (mc % 2)]
                dpf = mc % 3
                if mc >= 3:
                    mkwait("gpsimd", sT, dn_mmk[(ti, mc - 3)])
                elif ti > 0:
                    prev_mc = {0: 30, 1: 31, 2: 29}[mc]
                    mkwait("gpsimd", sT, dn_mmk[(ti - 1, prev_mc)])
                else:
                    mkwait("gpsimd", sG, sG.n)
                mkinc("gpsimd", sQd[dpf], 16, lambda e, mc=mc, dpf=dpf: e.dma_start(
                    out=dbuf[dpf][:],
                    in_=down_sc.rearrange("m h p d -> m p h d")[mc]))
                if mc >= 2:
                    mkwait("tensor", sC, silu_mk[(ti, mc - 2)])
                    mkwait("tensor", sV, htmp_mk[(ti, mc - 2)])
                elif ti > 0:
                    mkwait("tensor", sC, silu_mk[(ti - 1, MC - 2 + mc)])
                    mkwait("tensor", sV, htmp_mk[(ti - 1, MC - 2 + mc)])

                def gu_mm(e, pbf=pbf, sub=sub, bank=bank):
                    o0 = sub * 128
                    last = None
                    for k in range(KC):
                        gh = gbuf[pbf].ap()[:, k, 0, o0:o0 + 128]
                        gl = gbuf[pbf].ap()[:, k, 1, o0:o0 + 128]
                        uh_ = ubuf[pbf].ap()[:, k, 0, o0:o0 + 128]
                        ul_ = ubuf[pbf].ap()[:, k, 1, o0:o0 + 128]
                        zch = zh.ap()[:, k, :]
                        zcl = zl.ap()[:, k, :]
                        e.matmul(bank[:, 0:NT], gh, zch, start=(k == 0), stop=False)
                        e.matmul(bank[:, 0:NT], gh, zcl, start=False, stop=False)
                        e.matmul(bank[:, 0:NT], gl, zch, start=False, stop=False)
                        e.matmul(bank[:, NT:2 * NT], uh_, zch, start=False,
                                 stop=False)
                        e.matmul(bank[:, NT:2 * NT], uh_, zcl, start=False,
                                 stop=False)
                        last = e.matmul(bank[:, NT:2 * NT], ul_, zch, start=False,
                                        stop=(k == KC - 1))
                    return last
                gu_mmk[(ti, si)] = mkinc("tensor", sT, 1, gu_mm)
                mkwait("scalar", sT, gu_mmk[(ti, si)])
                if mc >= 2:
                    mkwait("scalar", sV, htmp_mk[(ti, mc - 2)])
                elif ti > 0:
                    mkwait("scalar", sV, htmp_mk[(ti - 1, MC - 2 + mc)])
                smk = mkinc("scalar", sC, 1, lambda e, mc=mc, bank=bank:
                            e.activation(sgb[mc % 2][:], bank[:, 0:NT], ACT.Silu,
                                         bias=bgcol[:, mc:mc + 1]))
                silu_mk[(ti, mc)] = smk
                mkwait("vector", sC, smk)
                htmp_mk[(ti, mc)] = mkinc(
                    "vector", sV, 1, lambda e, mc=mc, bank=bank:
                    e.scalar_tensor_tensor(htmp[mc % 2][:], bank[:, NT:2 * NT],
                                           bucol[:, mc:mc + 1], sgb[mc % 2][:],
                                           ALU.add, ALU.mult))
                ring_prev = (dn_mmk[(ti, mc - 8)] if mc >= 8 else
                             (dn_mmk[(ti - 1, MC - 8 + mc)] if ti > 0 else 0))
                mkwait("scalar", sV, htmp_mk[(ti, mc)])
                mkwait("scalar", sT, ring_prev)
                hhc_mk[(ti, mc)] = mkinc(
                    "scalar", sC, 1, lambda e, mc=mc: e.activation(
                        hh.ap()[:, mc % 8, :], htmp[mc % 2][:], ACT.Copy))
                mkwait("vector", sC, hhc_mk[(ti, mc)])
                mkwait("vector", sT, ring_prev)
                hls_mk[(ti, mc)] = mkinc(
                    "vector", sV, 1, lambda e, mc=mc: e.tensor_sub(
                        hl.ap()[:, mc % 8, :], htmp[mc % 2][:],
                        hh.ap()[:, mc % 8, :]))
                if mc >= 1:
                    emit_dn(ti, mc - 1)
        emit_dn(ti, MC - 1)

        if ti + 1 < NTILE:
            p2_zcast(ti + 1)
        p2_v1(ti)
        p2_ln1(ti)
        p2_ucast(ti)
        p2_ffn(ti)
        p2_v2(ti)

    wv_ln2_a(NTILE - 1)
    wv_ln2_b(NTILE - 1)
    wv_ln2_c(NTILE - 1)
    wv_ln2_d(NTILE - 1)
    wv_out(NTILE - 1, 0)
    wv_out(NTILE - 1, 1)

    mkwait("sync", sOo[0], sOo[0].n)
    mkwait("sync", sOo[1], sOo[1].n)

    by_eng = {"sync": [], "scalar": [], "vector": [], "tensor": [], "gpsimd": []}
    for name, fn in sched:
        by_eng[name].append(fn)
    block.sync(lambda e: [fn(e) for fn in by_eng["sync"]])
    block.scalar(lambda e: [fn(e) for fn in by_eng["scalar"]])
    block.vector(lambda e: [fn(e) for fn in by_eng["vector"]])
    block.tensor(lambda e: [fn(e) for fn in by_eng["tensor"]])
    block.gpsimd(lambda e: [fn(e) for fn in by_eng["gpsimd"]])
    ctx.close()

    _simulate(simlog)
    return nc


def _simulate(simlog):
    from collections import defaultdict, deque
    qs = defaultdict(deque)
    for eng, kind, semid, val in simlog:
        qs[eng].append((kind, semid, val))
    cur = defaultdict(int)
    progress = True
    while progress:
        progress = False
        for eng in list(qs):
            q = qs[eng]
            while q:
                kind, semid, val = q[0]
                if kind == "i":
                    cur[semid] = max(cur[semid], val)
                elif cur[semid] < val:
                    break
                q.popleft()
                progress = True
    stuck = {e: len(q) for e, q in qs.items() if q}
    assert not stuck, f"schedule deadlock, remaining: {stuck}"


_NC_CACHE = None


def kernel(L0_attn_x, L0_attn_norm_w, L0_attn_conv_w, L0_attn_conv_b, L0_attn_a,
           L0_attn_b, L0_attn_mlp_norm_w, L0_attn_w_gate, L0_attn_b_gate,
           L0_attn_w_up, L0_attn_b_up, L0_attn_w_down, L0_attn_b_down,
           L0_ffn_weight):
    global _NC_CACHE
    x = np.ascontiguousarray(np.asarray(L0_attn_x, dtype=np.float32).reshape(T_FULL, D))
    cvec = np.zeros((D, 8), np.float32)
    cvec[:, 0] = np.asarray(L0_attn_norm_w, np.float32)
    cw = np.asarray(L0_attn_conv_w, np.float32).reshape(D, 3)
    cvec[:, 1], cvec[:, 2], cvec[:, 3] = cw[:, 0], cw[:, 1], cw[:, 2]
    cvec[:, 4] = np.asarray(L0_attn_conv_b, np.float32)
    cvec[:, 5] = np.asarray(L0_attn_a, np.float32)
    cvec[:, 6] = np.asarray(L0_attn_b, np.float32)
    cvec[:, 7] = np.asarray(L0_attn_b_down, np.float32)
    bgu = np.ascontiguousarray(np.stack(
        [np.asarray(L0_attn_b_gate, np.float32),
         np.asarray(L0_attn_b_up, np.float32)], axis=1))
    mnw = np.ascontiguousarray(np.asarray(L0_attn_mlp_norm_w, np.float32)[None, :])
    ident = np.eye(128, dtype=np.float32)
    wg = np.ascontiguousarray(np.asarray(L0_attn_w_gate, np.float32))
    wu = np.ascontiguousarray(np.asarray(L0_attn_w_up, np.float32))
    wd = np.ascontiguousarray(np.asarray(L0_attn_w_down, np.float32))
    wf = np.ascontiguousarray(np.asarray(L0_ffn_weight, np.float32))

    if _NC_CACHE is None:
        _NC_CACHE = build_program()
    nc = _NC_CACHE

    in_maps = []
    for c in range(NCORE):
        s0 = c * NTOK
        e0 = s0 + NTOK
        xpad = np.zeros((PRE, D), np.float32)
        if s0 > 0:
            xpad[PRE - s0:] = x[:s0]
        in_maps.append({
            "xm": np.ascontiguousarray(x[s0:e0]),
            "xp": xpad,
            "xprev": np.ascontiguousarray(x[s0 - 1:s0]) if s0 > 0
                     else np.zeros((1, D), np.float32),
            "xnext": np.ascontiguousarray(x[e0:e0 + 1]) if e0 < T_FULL
                     else np.zeros((1, D), np.float32),
            "w_gate": wg, "w_up": wu, "w_down": wd, "w_ffn": wf,
            "cvec": cvec, "bgu": bgu, "mnw": mnw, "ident": ident,
        })
    res = run_bass_kernel_spmd(nc, in_maps, core_ids=list(range(NCORE)))
    out = np.concatenate([res.results[c]["y"] for c in range(NCORE)], axis=0)
    return out.reshape(1, T_FULL, D)


# version 30
# speedup vs baseline: 1.0416x; 1.0416x over previous
import numpy as np
import concourse.bass as bass
import concourse.mybir as mybir
from concourse.bass_utils import run_bass_kernel_spmd

F32 = mybir.dt.float32
BF16 = mybir.dt.bfloat16
ALU = mybir.AluOpType
ACT = mybir.ActivationFunctionType
AXX = mybir.AxisListType.X

NCORE = 8
T_FULL = 16384
D = 1024
H = 4096
NTOK = T_FULL // NCORE
NT = 256
NTILE = NTOK // NT
PRE = T_FULL - NTOK
PREB = PRE // NT
EPS_RMS = 1e-6
EPS_LN = 1e-5
EPS_TERN = 1e-5
KC = 8
MC = H // 128
MSTRIPE = 256
NSTRIPE = H // MSTRIPE
_DBG = False
_DBG_TILE = 0
_DBG_STALL = True
_RACECHECK = False


class Ct:
    def __init__(self, sem):
        self.sem = sem
        self.n = 0


def build_program():
    from contextlib import ExitStack

    nc = bass.Bass("TRN2", target_bir_lowering=False,
                   detect_race_conditions=_RACECHECK)

    xm = nc.dram_tensor("xm", [NTOK, D], F32, kind="ExternalInput").ap()
    xp = nc.dram_tensor("xp", [PRE, D], F32, kind="ExternalInput").ap()
    xprev = nc.dram_tensor("xprev", [1, D], F32, kind="ExternalInput").ap()
    xnext = nc.dram_tensor("xnext", [1, D], F32, kind="ExternalInput").ap()
    w_gate = nc.dram_tensor("w_gate", [D, H], F32, kind="ExternalInput").ap()
    w_up = nc.dram_tensor("w_up", [D, H], F32, kind="ExternalInput").ap()
    w_down = nc.dram_tensor("w_down", [H, D], F32, kind="ExternalInput").ap()
    w_ffn = nc.dram_tensor("w_ffn", [D, D], F32, kind="ExternalInput").ap()
    cvec = nc.dram_tensor("cvec", [D, 8], F32, kind="ExternalInput").ap()
    bgu = nc.dram_tensor("bgu", [H, 2], F32, kind="ExternalInput").ap()
    mnw_in = nc.dram_tensor("mnw", [1, D], F32, kind="ExternalInput").ap()
    ident_in = nc.dram_tensor("ident", [128, 128], F32, kind="ExternalInput").ap()
    y = nc.dram_tensor("y", [NTOK, D], F32, kind="ExternalOutput").ap()
    dbg = {}
    if _DBG:
        for nm in ("d_A", "d_conv", "d_h", "d_xmid", "d_w1", "d_z", "d_v1",
                   "d_u", "d_v2", "d_carry", "d_zh", "d_hh"):
            dbg[nm] = nc.dram_tensor(nm, [128, 8 * 258], F32,
                                     kind="ExternalOutput").ap()

    gate_sc = nc.dram_tensor("gate_sc", [KC, 2, 128, H], BF16, kind="Internal").ap()
    up_sc = nc.dram_tensor("up_sc", [KC, 2, 128, H], BF16, kind="Internal").ap()
    down_sc = nc.dram_tensor("down_sc", [MC, 2, 128, D], BF16, kind="Internal").ap()
    ffnT_sc = nc.dram_tensor("ffnT_sc", [KC, 128, D], BF16, kind="Internal").ap()

    sched = []
    simlog = []

    def mkwait(eng, ct, val=None):
        v = ct.n if val is None else val
        assert v <= ct.n, "wait beyond planned total"
        if v <= 0:
            return
        simlog.append((eng, "w", id(ct), v))
        sched.append((eng, lambda e, s=ct.sem, v=v: e.wait_ge(s, v)))

    def mkinc(eng, ct, amt, fn):
        ct.n += amt
        simlog.append((eng, "i", id(ct), ct.n))
        sched.append((eng, lambda e, fn=fn, s=ct.sem, a=amt: fn(e).then_inc(s, a)))
        return ct.n

    def mkop(eng, fn):
        sched.append((eng, lambda e, fn=fn: fn(e)))

    ctx = ExitStack()
    sb = lambda n, s, d: ctx.enter_context(nc.sbuf_tensor(n, s, d))
    psal = lambda n: ctx.enter_context(nc.psum_tensor(n, [128, 512], F32))
    mksem = lambda n: ctx.enter_context(nc.semaphore(n))

    sQc = Ct(mksem("sQc"))
    sQt = Ct(mksem("sQt"))
    sQw = [Ct(mksem(f"sQw{i}")) for i in range(2)]
    sQx = [Ct(mksem(f"sQx{i}")) for i in range(2)]
    sQh = Ct(mksem("sQh"))
    sQg = [Ct(mksem(f"sQg{i}")) for i in range(2)]
    sQu = [Ct(mksem(f"sQu{i}")) for i in range(2)]
    sQd = [Ct(mksem(f"sQd{i}")) for i in range(3)]
    sQf = [Ct(mksem(f"sQf{i}")) for i in range(2)]
    sOo = [Ct(mksem(f"sOo{i}")) for i in range(2)]
    sG = Ct(mksem("sG"))
    sT = Ct(mksem("sT"))
    sV = Ct(mksem("sV"))
    sC = Ct(mksem("sC"))

    ident_s = sb("ident_s", [128, 128], F32)
    ones_c16 = sb("ones_c16", [128, 1], BF16)
    ones_row = sb("ones_row", [1, 128], F32)
    mnw_row = sb("mnw_row", [1, D], F32)
    cv = sb("cv", [128, 8, 8], F32)
    bnw = sb("bnw", [128, 8], F32)
    w0c = sb("w0c", [128, 8], F32)
    w1c = sb("w1c", [128, 8], F32)
    w2c = sb("w2c", [128, 8], F32)
    svec = sb("svec", [128, 8], F32)
    bgcol = sb("bgcol", [128, MC], F32)
    bucol = sb("bucol", [128, MC], F32)
    scale_col = sb("scale_col", [128, 1], F32)
    rscale_col = sb("rscale_col", [128, 1], F32)
    tabs = sb("tabs", [128, 8], F32)
    sS = sb("sS", [128, 8, 512], F32)
    ones512 = sb("ones512", [128, 512], F32)
    gcarry = sb("gcarry", [128, 8], F32)
    stx = [sb(f"stx{i}", [128, D], F32) for i in range(2)]
    stxr = [sb(f"stxr{i}", [128, D], F32) for i in range(2)]
    rcol = sb("rcol", [128, 1], F32)
    rcol1 = sb("rcol1", [128, 1], F32)
    A = sb("A", [128, 8, 258], F32)
    X = sb("X", [128, 8, NT], F32)
    M = sb("M", [128, 8, NT], F32)
    V = sb("V", [128, 8, NT], F32)
    W = [sb(f"W{i}", [128, 8, NT], F32) for i in range(2)]
    zh = sb("zh", [128, 8, NT], BF16)
    zl = sb("zl", [128, 8, NT], BF16)
    lnh = sb("lnh", [128, 8, NT], BF16)
    lnl = sb("lnl", [128, 8, NT], BF16)
    uhb = sb("uhb", [128, 8, NT], BF16)
    ulb = sb("ulb", [128, 8, NT], BF16)
    hh = sb("hh", [128, 8, NT], BF16)
    hl = sb("hl", [128, 8, NT], BF16)
    sgb = [sb(f"sgb{i}", [128, NT], F32) for i in range(2)]
    htmp = [sb(f"htmp{i}", [128, NT], F32) for i in range(2)]
    gbuf = [sb(f"gbuf{i}", [128, KC, 2, MSTRIPE], BF16) for i in range(2)]
    ubuf = [sb(f"ubuf{i}", [128, KC, 2, MSTRIPE], BF16) for i in range(2)]
    dbuf = [sb(f"dbuf{i}", [128, 2, D], BF16) for i in range(3)]
    fbuf = [sb(f"fbuf{i}", [128, D], BF16) for i in range(2)]
    outst = [sb(f"outst{i}", [128, D], F32) for i in range(2)]
    hrow = sb("hrow", [1, D], F32)
    hjunk = sb("hjunk", [1, D], F32)
    hsr = sb("hsr", [1, 1], F32)
    st_a = sb("st_a", [1, NT], F32)
    st_b = sb("st_b", [1, NT], F32)
    st_c = sb("st_c", [1, NT], F32)
    eps_rms_c = sb("eps_rms_c", [128, 1], F32)
    eps_ln_c = sb("eps_ln_c", [128, 1], F32)
    negh_c = sb("negh_c", [128, 1], F32)
    posh_c = sb("posh_c", [128, 1], F32)
    dbg_misc = sb("dbg_misc", [128, 16], F32)
    dbg_snap = sb("dbg_snap", [128, 96], F32)
    pb = [psal(f"pb{i}") for i in range(8)]

    sS_f = sS.ap().rearrange("p c t -> p (c t)")
    tern = sS.ap().rearrange("p c t -> p (c t)").rearrange(
        "p (a i) -> p a i", i=D)
    wbufh = [sS.ap()[:, 0:4, :].rearrange("p c t -> p (c t)"),
             sS.ap()[:, 4:8, :].rearrange("p c t -> p (c t)")]
    whi = hh.ap().rearrange("p m t -> p (m t)")
    wlo = hl.ap().rearrange("p m t -> p (m t)")
    fevb = lnh.ap().rearrange("p c t -> p (c t)")[:, 0:512]

    ZH = zh.ap().rearrange("p c t -> p (c t)")
    ZL = zl.ap().rearrange("p c t -> p (c t)")
    LNHF = lnh.ap().rearrange("p c t -> p (c t)")
    LNLF = lnl.ap().rearrange("p c t -> p (c t)")
    UHF = uhb.ap().rearrange("p c t -> p (c t)")
    ULF = ulb.ap().rearrange("p c t -> p (c t)")
    XF = X.ap().rearrange("p c t -> p (c t)")
    MF = M.ap().rearrange("p c t -> p (c t)")
    VF = V.ap().rearrange("p c t -> p (c t)")
    WF = [W[i].ap().rearrange("p c t -> p (c t)") for i in range(2)]


    def snap(i):
        return
        def _s(e, i=i):
            c0 = i * 16
            e.tensor_copy(dbg_snap[:, c0:c0 + 1], scale_col[:])
            e.tensor_copy(dbg_snap[:, c0 + 1:c0 + 2], rscale_col[:])
            e.tensor_copy(dbg_snap[:, c0 + 2:c0 + 10], tabs[:])
            e.tensor_copy(dbg_snap[:, c0 + 10:c0 + 11], negh_c[:])
            e.tensor_copy(dbg_snap[:, c0 + 11:c0 + 12], posh_c[:])
            e.tensor_copy(dbg_snap[:, c0 + 12:c0 + 13], eps_rms_c[:])
            return e.tensor_copy(dbg_snap[:, c0 + 13:c0 + 14], eps_ln_c[:])
        mkinc("vector", sV, 1, _s)

    block = ctx.enter_context(nc.Block())

    mkinc("sync", sQc, 16, lambda e: e.dma_start(out=ident_s[:], in_=ident_in))
    mkinc("sync", sQc, 16, lambda e: e.dma_start(out=mnw_row[:], in_=mnw_in))
    mkinc("sync", sQc, 16, lambda e: e.dma_start(
        out=cv[:], in_=cvec.rearrange("(c p) k -> p c k", p=128)))
    mkinc("sync", sQc, 16, lambda e: e.dma_start(
        out=bgcol[:], in_=bgu.rearrange("(m p) k -> p m k", p=128)[:, :, 0],
        allow_slow_non_contiguous=True))
    mkinc("sync", sQc, 16, lambda e: e.dma_start(
        out=bucol[:], in_=bgu.rearrange("(m p) k -> p m k", p=128)[:, :, 1],
        allow_slow_non_contiguous=True))
    mkwait("vector", sQc)

    def p0_consts(e):
        e.memset(ones_c16[:], 1.0)
        e.memset(ones_row[:], 1.0)
        e.memset(ones512[:], 1.0)
        e.memset(gcarry[:], 0.0)
        e.memset(eps_rms_c[:], EPS_RMS)
        e.memset(eps_ln_c[:], EPS_LN)
        e.memset(negh_c[:], -0.5)
        e.memset(posh_c[:], 0.5)
        nw = cv.ap()[:, :, 0]
        e.tensor_mul(w0c[:], cv.ap()[:, :, 1], nw)
        e.tensor_mul(w1c[:], cv.ap()[:, :, 2], nw)
        e.tensor_mul(w2c[:], cv.ap()[:, :, 3], nw)
        return e.tensor_mul(bnw[:], cv.ap()[:, :, 6], nw)
    mkinc("vector", sV, 1, p0_consts)
    mkwait("scalar", sV)
    mkinc("scalar", sC, 1, lambda e: e.activation(svec[:], cv.ap()[:, :, 5], ACT.Tanh))
    mkwait("vector", sC, sC.n)
    snap(0)

    for h in range(2):
        if h == 1:
            mkwait("sync", sC, sC.n)
        mkinc("sync", sQt, 16, lambda e, h=h: e.dma_start(
            out=tern, in_=w_ffn[h * 512:(h + 1) * 512, :].rearrange(
                "(a p) i -> p a i", p=128)))
        mkwait("scalar", sQt)

        def absp(e, h=h):
            junk = [stx[0][:], stx[1][:], XF[:, 0:1024], XF[:, 1024:2048],
                    VF[:, 0:1024], VF[:, 1024:2048], MF[:, 0:1024],
                    MF[:, 1024:2048]]
            last = None
            for a2 in range(4):
                last = e.activation(junk[h * 4 + a2], tern[:, a2, :], ACT.Abs,
                                    accum_out=tabs[:, h * 4 + a2:h * 4 + a2 + 1])
            return last
        mkinc("scalar", sC, 1, absp)
    mkwait("vector", sC, sC.n)
    mkinc("vector", sV, 1, lambda e: e.tensor_reduce(rcol[:], tabs[:], AXX, ALU.add))
    mkwait("tensor", sV)
    trm = mkinc("tensor", sT, 1, lambda e: e.matmul(
        pb[6][0:1, 0:128], rcol[:], ones512[:, 0:128], start=True, stop=True))
    mkwait("vector", sT, trm)

    def p0_scale(e):
        e.tensor_scalar(st_b[0:1, 0:128], pb[6][0:1, 0:128], 1.0 / (D * D),
                        None, ALU.mult)
        e.drain()
        e.tensor_scalar(st_b[0:1, 0:128], st_b[0:1, 0:128], EPS_TERN,
                        None, ALU.max)
        e.drain()
        e.reciprocal(st_c[0:1, 0:128], st_b[0:1, 0:128])
        e.drain()
        return e.tensor_scalar(st_b[0:1, 0:128], st_b[0:1, 0:128], 0.5,
                               None, ALU.mult)
    mkinc("vector", sV, 1, p0_scale)
    mkwait("tensor", sV)
    bctm = mkinc("tensor", sT, 1, lambda e: (
        e.transpose(pb[7][:, 0:1], st_b[0:1, 0:128], ident_s[0:1, 0:1]),
        e.transpose(pb[7][:, 128:129], st_c[0:1, 0:128], ident_s[0:1, 0:1]))[-1])
    mkwait("vector", sT, bctm)
    mkinc("vector", sV, 1, lambda e: (
        e.tensor_copy(scale_col[:], pb[7][:, 0:1]),
        e.tensor_copy(rscale_col[:], pb[7][:, 128:129]))[-1])
    snap(1)

    fmark = {}
    for h in range(2):
        mkwait("sync", sV, sV.n)
        mkinc("sync", sQt, 16, lambda e, h=h: e.dma_start(
            out=tern, in_=w_ffn[h * 512:(h + 1) * 512, :].rearrange(
                "(a p) i -> p a i", p=128)))
        mkwait("vector", sQt)

        for a2 in range(4):
            uv = mkinc("vector", sV, 1, lambda e, a2=a2: e.tensor_scalar(
                tern[:, a2, :], tern[:, a2, :], rscale_col[:, 0:1], None, ALU.mult))
            mkwait("scalar", sV, uv)

            def signs(e, a2=a2):
                e.activation(stx[0][:], tern[:, a2, :], ACT.Sign,
                             bias=negh_c[:, 0:1])
                return e.activation(stx[1][:], tern[:, a2, :], ACT.Sign,
                                    bias=posh_c[:, 0:1])
            sgm = mkinc("scalar", sC, 1, signs)
            mkwait("vector", sC, sgm)

            def combine(e, a2=a2):
                return e.tensor_add(tern[:, a2, :], stx[0][:], stx[1][:])
            mkinc("vector", sV, 1, combine)
        for ic in range(8):
            mkwait("tensor", sV)

            def ftp(e, h=h, ic=ic):
                last = None
                for a2 in range(4):
                    bank = pb[a2 // 2]
                    last = e.transpose(
                        bank[:, (a2 % 2) * 128:(a2 % 2) * 128 + 128],
                        tern[:, a2, ic * 128:(ic + 1) * 128], ident_s[:])
                return last
            tmk = mkinc("tensor", sT, 1, ftp)
            mkwait("vector", sT, tmk)
            if (h, ic) != (0, 0):
                mkwait("vector", sG, sG.n)

            def fev(e):
                e.tensor_copy(fevb[:, 0:256], pb[0][:, 0:256])
                return e.tensor_copy(fevb[:, 256:512], pb[1][:, 0:256])
            vmk = mkinc("vector", sV, 1, fev)
            mkwait("gpsimd", sV, vmk)
            fmark[(h, ic)] = mkinc("gpsimd", sG, 16, lambda e, h=h, ic=ic: e.dma_start(
                out=ffnT_sc[ic][:, h * 512:(h + 1) * 512], in_=fevb))

    snap(2)
    jobs = []
    for k in range(KC):
        for h in range(2):
            jobs.append((w_gate[k * 128:(k + 1) * 128, h * 2048:(h + 1) * 2048],
                         gate_sc[k, 0][:, h * 2048:(h + 1) * 2048],
                         gate_sc[k, 1][:, h * 2048:(h + 1) * 2048]))
    for k in range(KC):
        for h in range(2):
            jobs.append((w_up[k * 128:(k + 1) * 128, h * 2048:(h + 1) * 2048],
                         up_sc[k, 0][:, h * 2048:(h + 1) * 2048],
                         up_sc[k, 1][:, h * 2048:(h + 1) * 2048]))
    for j2 in range(MC // 2):
        jobs.append((w_down[j2 * 256:(j2 + 1) * 256, :].rearrange(
                        "(a p) d -> p a d", p=128),
                     down_sc[j2 * 2:(j2 + 1) * 2, 0].rearrange("m p d -> p m d"),
                     down_sc[j2 * 2:(j2 + 1) * 2, 1].rearrange("m p d -> p m d")))
    cmark, vmark, gmark = {}, {}, {}

    def emit_cast_job(j):
        src, dhi, dlo = jobs[j]
        buf = WF[j % 2]
        three_d = len(src.shape) == 3
        bufd = buf.rearrange("p (a d) -> p a d", d=D) if three_d else buf
        whid = whi.rearrange("p (a d) -> p a d", d=D) if three_d else whi
        wlod = wlo.rearrange("p (a d) -> p a d", d=D) if three_d else wlo
        if j >= 2:
            mkwait("gpsimd", sC, cmark[j - 2])
            mkwait("gpsimd", sV, vmark[j - 2])
        mkinc("gpsimd", sQw[j % 2], 16, lambda e, src=src, bufd=bufd: e.dma_start(
            out=bufd, in_=src))
        mkwait("scalar", sQw[j % 2])
        if j >= 1:
            mkwait("scalar", sG, gmark[j - 1])
        cmark[j] = mkinc("scalar", sC, 1, lambda e, buf=buf: e.activation(
            whi, buf, ACT.Copy))
        mkwait("vector", sC, cmark[j])
        if j >= 1:
            mkwait("vector", sG, gmark[j - 1])
        vmark[j] = mkinc("vector", sV, 1, lambda e, buf=buf: e.tensor_sub(
            wlo, buf, whi))
        mkwait("gpsimd", sC, cmark[j])
        mkinc("gpsimd", sG, 16, lambda e, dhi=dhi, whid=whid: e.dma_start(
            out=dhi, in_=whid))
        mkwait("gpsimd", sV, vmark[j])
        gmark[j] = mkinc("gpsimd", sG, 16, lambda e, dlo=dlo, wlod=wlod: e.dma_start(
            out=dlo, in_=wlod))

    mkwait("vector", sT, sT.n)
    mkwait("vector", sC, sC.n)
    mkwait("vector", sG, sG.n)

    def sbcast(e):
        last = None
        for c in range(8):
            last = e.tensor_scalar(sS.ap()[:, c, :], ones512[:], svec[:, c:c + 1],
                                   None, ALU.mult)
        return last
    mkinc("vector", sV, 1, sbcast)

    tpmk, sqmk, xrmk = {}, {}, {}
    last_xr = [0]
    rcolp = [rcol, rcol1]

    def emit_group(gi, src_ap):
        buf = stx[gi % 2]
        sr = stxr[gi % 2]
        rc = rcolp[gi % 2]
        if gi >= 2:
            mkwait("sync", sC, sqmk[gi - 2])
            mkwait("sync", sV, xrmk[gi - 2])
            if (gi - 2) in tpmk:
                mkwait("sync", sT, tpmk[gi - 2])
        else:
            mkwait("sync", sC, sC.n)
            mkwait("sync", sV, sV.n)
        mkinc("sync", sQx[gi % 2], 16, lambda e, buf=buf, src_ap=src_ap: e.dma_start(
            out=buf[:], in_=src_ap))
        mkwait("scalar", sQx[gi % 2])
        if gi >= 2:
            mkwait("scalar", sT, tpmk[gi - 2])
            mkwait("scalar", sV, xrmk[gi - 2])
        mkop("scalar", lambda e, buf=buf, sr=sr, rc=rc: e.activation(
            sr[:], buf[:], ACT.Square, accum_out=rc[:]))
        mkop("scalar", lambda e: e.drain())
        mkop("scalar", lambda e, rc=rc: e.activation(rc[:], rc[:], ACT.Ln,
                                              bias=eps_rms_c[:, 0:1], scale=1.0 / D))
        mkop("scalar", lambda e: e.drain())
        sqmk[gi] = mkinc("scalar", sC, 1, lambda e, rc=rc: e.activation(
            rc[:], rc[:], ACT.Exp, scale=-0.5))
        mkwait("vector", sC, sqmk[gi])
        xrmk[gi] = mkinc("vector", sV, 1, lambda e, buf=buf, sr=sr, rc=rc:
                         e.tensor_scalar(sr[:], buf[:], rc[:, 0:1], None, ALU.mult))
        last_xr[0] = xrmk[gi]
        mkwait("tensor", sV, xrmk[gi])

        def tp(e, sr=sr):
            last = None
            for c in range(8):
                bank = pb[c // 4]
                last = e.transpose(bank[:, (c % 4) * 128:(c % 4) * 128 + 128],
                                   sr[:, c * 128:(c + 1) * 128], ident_s[:])
            return last
        tpmk[gi] = mkinc("tensor", sT, 1, tp)

    mkwait("scalar", sT, sT.n)

    ev_s = {}
    for blk in range(PREB):
        for g in range(2):
            gi = blk * 2 + g
            emit_group(gi, xp[gi * 128:(gi + 1) * 128, :])
            mkwait("scalar", sT, tpmk[gi])
            ev_s[gi] = mkinc("scalar", sC, 1, lambda e, g=g: e.activation(
                V.ap()[:, 0:4, g * 128:(g + 1) * 128],
                pb[0][:, 0:512].rearrange("p (c t) -> p c t", c=4), ACT.Copy))
            mkwait("vector", sT, tpmk[gi])

            def pre_ev(e, g=g):
                return e.tensor_copy(V.ap()[:, 4:8, g * 128:(g + 1) * 128],
                                     pb[1][:, 0:512].rearrange("p (c t) -> p c t", c=4))
            mkinc("vector", sV, 1, pre_ev)

        mkwait("vector", sC, ev_s[blk * 2 + 1])

        def pre_scan(e, blk=blk):
            e.drain()
            dst = X if blk % 2 == 0 else M
            prev = M if blk % 2 == 0 else X
            last = None
            for c in range(8):
                init = (gcarry[:, c:c + 1] if blk == 0
                        else prev.ap()[:, c, NT - 1:NT])
                last = e.tensor_tensor_scan(dst.ap()[:, c, :],
                                            sS.ap()[:, c, 0:NT],
                                            V.ap()[:, c, :], init,
                                            ALU.mult, ALU.add)
            return last
        mkinc("vector", sV, 1, pre_scan)
        if blk < len(jobs):
            emit_cast_job(blk)

    def pre_carry_out(e):
        src = X if (PREB - 1) % 2 == 0 else M
        last = None
        for c in range(8):
            last = e.tensor_copy(gcarry[:, c:c + 1], src.ap()[:, c, NT - 1:NT])
        return last
    mkinc("vector", sV, 1, pre_carry_out)

    g_sq, g_xr = {}, {}
    tpA, tpX, evA, evX, hTp = {}, {}, {}, {}, {}
    conv_mk, scan_mk, mid_mk = {}, {}, {}
    sqz_mk, stats_mk, rv_mk, r2_mk = {}, {}, {}, {}
    mb_mk, zm_mk, z_s, z_v = {}, {}, {}, {}
    v1_mk, v2_mk = {}, {}
    ln1_sum, ln1_mrow, ln1_mbc, ln1_ctr, ln1_sq, ln1_ssq, ln1_qbc, ln1_ap = \
        {}, {}, {}, {}, {}, {}, {}, {}
    ln2_sum, ln2_mrow, ln2_mbc, ln2_ctr, ln2_sq, ln2_ssq, ln2_qbc, ln2_ap, \
        ln2_rs = {}, {}, {}, {}, {}, {}, {}, {}, {}
    u_s, u_v = {}, {}
    otm, oev, out_omk = {}, {}, {}
    silu_mk, htmp_mk, hhc_mk, hls_mk = {}, {}, {}, {}
    halo_sq, halo_xr = {}, {}
    hstate = {"id": -1}
    rhalo_ev = {}
    gu_mmk, dn_mmk, ff_mmk = {}, {}, {}

    def kwarm():
        mkop("tensor", lambda e: e.matmul(pb[4][0:1, 0:128], ones512[:, 0:1],
                                          ident_s[:, 0:128],
                                          start=True, stop=True))

    def w_load(t, g):
        mkwait("sync", sC, g_sq.get((t - 1, g), 0))
        mkwait("sync", sV, g_xr.get((t - 1, g), 0))
        mkwait("sync", sT, tpX.get((t - 1, g), 0))
        src = xm[(t * NT + g * 128):(t * NT + (g + 1) * 128), :]
        mkinc("sync", sQx[g], 16, lambda e, src=src, g=g: e.dma_start(
            out=stx[g][:], in_=src))

    def w_rms(t, g):
        buf, sr = stx[g], stxr[g]
        mkwait("scalar", sQx[g])
        mkwait("scalar", sT, tpA.get((t - 1, g), 0))
        mkwait("scalar", sV, last_xr[0])
        mkop("scalar", lambda e: e.activation(sr[:], buf[:], ACT.Square,
                                              accum_out=rcol[:]))
        mkop("scalar", lambda e: e.drain())
        mkop("scalar", lambda e: e.activation(rcol[:], rcol[:], ACT.Ln,
                                              bias=eps_rms_c[:, 0:1], scale=1.0 / D))
        mkop("scalar", lambda e: e.drain())
        g_sq[(t, g)] = mkinc("scalar", sC, 1, lambda e: e.activation(
            rcol[:], rcol[:], ACT.Exp, scale=-0.5))

    def w_xr(t, g):
        buf, sr = stx[g], stxr[g]
        mkwait("vector", sC, g_sq[(t, g)])
        g_xr[(t, g)] = mkinc("vector", sV, 1, lambda e: e.tensor_scalar(
            sr[:], buf[:], rcol[:, 0:1], None, ALU.mult))
        last_xr[0] = g_xr[(t, g)]

    def w_halo_S(src_ap):
        hid = hstate["id"]
        mkwait("sync", sC, halo_sq.get(hid, 0))
        mkwait("sync", sV, halo_xr.get(hid, 0))
        mkwait("sync", sT, hTp.get(hid, 0))
        hstate["id"] = hid = hid + 1
        mkinc("sync", sQh, 16, lambda e: e.dma_start(out=hrow[:], in_=src_ap))
        mkwait("scalar", sQh)
        mkop("scalar", lambda e: e.activation(hjunk[:], hrow[:], ACT.Square,
                                              accum_out=hsr[:]))
        mkop("scalar", lambda e: e.drain())
        mkop("scalar", lambda e: e.activation(hsr[:], hsr[:], ACT.Ln,
                                              bias=eps_rms_c[0:1, 0:1],
                                              scale=1.0 / D))
        mkop("scalar", lambda e: e.drain())
        halo_sq[hid] = mkinc("scalar", sC, 1, lambda e: e.activation(
            hsr[:], hsr[:], ACT.Exp, scale=-0.5))

    def w_halo_xr():
        hid = hstate["id"]
        mkwait("vector", sC, halo_sq[hid])
        halo_xr[hid] = mkinc("vector", sV, 1, lambda e: e.tensor_scalar(
            hrow[:], hrow[:], hsr[0:1, 0:1], None, ALU.mult))

    def w_halo_tp_ev(dst_col, war_sv):
        hid = hstate["id"]
        mkwait("tensor", sV, halo_xr[hid])
        mkwait("tensor", sV, war_sv)

        def htp(e):
            last = None
            for c in range(8):
                last = e.transpose(pb[6][:, c:c + 1],
                                   hrow[0:1, c * 128:(c + 1) * 128],
                                   ident_s[0:1, 0:1])
            return last
        hTp[hid] = mkinc("tensor", sT, 1, htp)
        mkwait("vector", sT, hTp[hid])
        ev = mkinc("vector", sV, 1, lambda e, dst_col=dst_col: e.tensor_copy(
            A.ap()[:, :, dst_col:dst_col + 1].rearrange("p c t -> p (c t)"),
            pb[6][:, 0:8]))
        mkop("vector", lambda e: e.drain())
        return ev

    def w_lhcopy(t):
        mkop("vector", lambda e: e.tensor_copy(
            A.ap()[:, :, 0:1], A.ap()[:, :, 256:257]))

    def w_tpA(t, g, war_sv):
        mkwait("tensor", sV, g_xr[(t, g)])
        mkwait("tensor", sV, war_sv)

        def tp(e, g=g):
            last = None
            for c in range(8):
                bank = pb[6 + c // 4]
                last = e.transpose(bank[:, (c % 4) * 128:(c % 4) * 128 + 128],
                                   stxr[g][:, c * 128:(c + 1) * 128], ident_s[:])
            return last
        tpA[(t, g)] = mkinc("tensor", sT, 1, tp)

    def w_evA(t, g):
        mkwait("vector", sT, tpA[(t, g)])

        def ev(e, g=g):
            e.tensor_copy(A.ap()[:, 0:4, 1 + g * 128:1 + (g + 1) * 128],
                          pb[6][:, 0:512].rearrange("p (c t) -> p c t", c=4))
            return e.tensor_copy(A.ap()[:, 4:8, 1 + g * 128:1 + (g + 1) * 128],
                                 pb[7][:, 0:512].rearrange("p (c t) -> p c t", c=4))
        evA[(t, g)] = mkinc("vector", sV, 1, ev)

    def w_tpX(t, g, war_sv):
        mkwait("tensor", sV, war_sv)

        def tp(e, g=g):
            last = None
            for c in range(8):
                bank = pb[6 + c // 4]
                last = e.transpose(bank[:, (c % 4) * 128:(c % 4) * 128 + 128],
                                   stx[g][:, c * 128:(c + 1) * 128], ident_s[:])
            return last
        tpX[(t, g)] = mkinc("tensor", sT, 1, tp)

    def w_evX(t, g):
        mkwait("vector", sT, tpX[(t, g)])

        def ev(e, g=g):
            e.tensor_copy(X.ap()[:, 0:4, g * 128:(g + 1) * 128],
                          pb[6][:, 0:512].rearrange("p (c t) -> p c t", c=4))
            return e.tensor_copy(X.ap()[:, 4:8, g * 128:(g + 1) * 128],
                                 pb[7][:, 0:512].rearrange("p (c t) -> p c t", c=4))
        evX[(t, g)] = mkinc("vector", sV, 1, ev)

    def w_conv(t):
        mkwait("vector", sT, otm.get((t - 2, 1), 0))

        def conv_chunk(e):
            last = None
            for c in range(8):
                e.tensor_scalar(M.ap()[:, c, :], A.ap()[:, c, 2:258],
                                w2c[:, c:c + 1], cv.ap()[:, c, 4:5],
                                ALU.mult, ALU.add)
            for c in range(8):
                e.scalar_tensor_tensor(M.ap()[:, c, :], A.ap()[:, c, 0:256],
                                       w0c[:, c:c + 1], M.ap()[:, c, :],
                                       ALU.mult, ALU.add)
            for c in range(8):
                last = e.scalar_tensor_tensor(M.ap()[:, c, :], A.ap()[:, c, 1:257],
                                              w1c[:, c:c + 1], M.ap()[:, c, :],
                                              ALU.mult, ALU.add)
            return last
        conv_mk[t] = mkinc("vector", sV, 1, conv_chunk)

    def w_scan(t):
        mkwait("vector", sT, ln2_sum.get(t - 2, 0))

        def tile_scan(e):
            for c in range(8):
                e.tensor_tensor_scan(V.ap()[:, c, :], sS.ap()[:, c, 0:NT],
                                     A.ap()[:, c, 1:257], gcarry[:, c:c + 1],
                                     ALU.mult, ALU.add)
            last = None
            for c in range(8):
                last = e.tensor_copy(gcarry[:, c:c + 1], V.ap()[:, c, NT - 1:NT])
            return last
        scan_mk[t] = mkinc("vector", sV, 1, tile_scan)

    def w_mid(t):
        def mid_ops(e):
            last = None
            for c in range(8):
                e.scalar_tensor_tensor(M.ap()[:, c, :], V.ap()[:, c, :],
                                       bnw[:, c:c + 1], M.ap()[:, c, :],
                                       ALU.mult, ALU.add)
            for c in range(8):
                e.tensor_add(M.ap()[:, c, :], M.ap()[:, c, :], X.ap()[:, c, :])
            for c in range(8):
                last = e.scalar_tensor_tensor(W[t % 2].ap()[:, c, :],
                                              X.ap()[:, c, :], cv.ap()[:, c, 7:8],
                                              M.ap()[:, c, :], ALU.add, ALU.add)
            return last
        mid_mk[t] = mkinc("vector", sV, 1, mid_ops)

    def w_rms2(t):
        mkwait("scalar", sV, mid_mk[t])
        mkwait("scalar", sT, stats_mk.get(t - 1, 0))
        sqz_mk[t] = mkinc("scalar", sC, 1, lambda e: e.activation(
            LNHF, MF, ACT.Square))
        mkwait("tensor", sC, sqz_mk[t])
        mkwait("tensor", sV, rhalo_ev[t])

        def rms2_stats(e):
            last = None
            for c in range(8):
                last = e.matmul(pb[6][0:1, 0:NT], ones_c16[:], lnh.ap()[:, c, :],
                                start=(c == 0), stop=(c == 7))
            return last
        stats_mk[t] = mkinc("tensor", sT, 1, rms2_stats)
        mkwait("vector", sT, stats_mk[t])
        rv_mk[t] = mkinc("vector", sV, 1, lambda e: e.tensor_scalar(
            st_a[:], pb[6][0:1, 0:NT], 1.0 / D, None, ALU.mult))
        mkwait("scalar", sV, rv_mk[t])
        mkop("scalar", lambda e: e.activation(st_a[:], st_a[:], ACT.Ln,
                                              bias=eps_rms_c[0:1, 0:1]))
        mkop("scalar", lambda e: e.drain())
        r2_mk[t] = mkinc("scalar", sC, 1, lambda e: e.activation(
            st_a[:], st_a[:], ACT.Exp, scale=-0.5))

    def w_mbz(t, wave):
        if wave == 0:
            mkwait("tensor", sC, r2_mk[t])
            mkwait("tensor", sV, rv_mk[t])
        else:
            mkwait("tensor", sV, zm_mk[(t, 0)])

        def mb(e, wave=wave):
            last = None
            for i in range(4):
                c = wave * 4 + i
                bank = pb[6 + i // 2]
                o0 = (i % 2) * NT
                last = e.matmul(bank[:, o0:o0 + NT],
                                mnw_row[0:1, c * 128:(c + 1) * 128], st_a[:],
                                start=(i % 2 == 0), stop=(i % 2 == 1))
            return last
        mb_mk[(t, wave)] = mkinc("tensor", sT, 1, mb)
        mkwait("vector", sT, mb_mk[(t, wave)])

        def zmul(e, wave=wave):
            last = None
            for i in range(4):
                c = wave * 4 + i
                bank = pb[6 + i // 2]
                o0 = (i % 2) * NT
                last = e.tensor_mul(M.ap()[:, c, :], M.ap()[:, c, :],
                                    bank[:, o0:o0 + NT])
            return last
        zm_mk[(t, wave)] = mkinc("vector", sV, 1, zmul)

    def p2_zcast(t):
        mkwait("scalar", sV, zm_mk[(t, 1)])
        if t > 0:
            mkwait("scalar", sT, gu_mmk[(t - 1, NSTRIPE - 1)])
        z_s[t] = mkinc("scalar", sC, 1, lambda e: e.activation(ZH, MF, ACT.Copy))
        mkwait("vector", sC, z_s[t])
        z_v[t] = mkinc("vector", sV, 1, lambda e: e.tensor_sub(ZL, MF, ZH))

    def p2_v1(t):
        mkwait("vector", sT, dn_mmk[(t, MC - 1)])

        def v1_ops(e):
            last = None
            for dc in range(8):
                last = e.tensor_add(W[t % 2].ap()[:, dc, :],
                                    pb[dc // 2][:, (dc % 2) * NT:(dc % 2 + 1) * NT],
                                    W[t % 2].ap()[:, dc, :])
            return last
        v1_mk[t] = mkinc("vector", sV, 1, v1_ops)

    def p2_ln1(t):
        wb = W[t % 2]
        mkwait("tensor", sV, v1_mk[t])

        def l1sum(e):
            last = None
            for c in range(8):
                last = e.matmul(pb[0][0:1, 0:NT], ones512[:, 0:1],
                                wb.ap()[:, c, :], start=(c == 0), stop=(c == 7))
            return last
        ln1_sum[t] = mkinc("tensor", sT, 1, l1sum)
        mkwait("vector", sT, ln1_sum[t])
        ln1_mrow[t] = mkinc("vector", sV, 1, lambda e: e.tensor_scalar(
            st_b[:], pb[0][0:1, 0:NT], 1.0 / D, None, ALU.mult))
        mkwait("tensor", sV, ln1_mrow[t])
        ln1_mbc[t] = mkinc("tensor", sT, 1, lambda e: e.matmul(
            pb[2][:, 0:NT], ones_row[:], st_b[:], start=True, stop=True))
        mkwait("vector", sT, ln1_mbc[t])

        def l1ctr(e):
            last = None
            for c in range(8):
                last = e.tensor_sub(wb.ap()[:, c, :], wb.ap()[:, c, :],
                                    pb[2][:, 0:NT])
            return last
        ln1_ctr[t] = mkinc("vector", sV, 1, l1ctr)
        mkwait("tensor", sV, ln1_ctr[t])
        kwarm()
        mkwait("scalar", sV, ln1_ctr[t])
        mkwait("scalar", sT, ln2_ssq.get(t - 1, 0))
        ln1_sq[t] = mkinc("scalar", sC, 1, lambda e: e.activation(
            LNLF, WF[t % 2], ACT.Square))
        mkwait("tensor", sC, ln1_sq[t])

        def l1ssq(e):
            last = None
            for c in range(8):
                last = e.matmul(pb[1][0:1, 0:NT], ones_c16[:], lnl.ap()[:, c, :],
                                start=(c == 0), stop=(c == 7))
            return last
        ln1_ssq[t] = mkinc("tensor", sT, 1, l1ssq)
        mkwait("vector", sT, ln1_ssq[t])
        vr = mkinc("vector", sV, 1, lambda e: e.tensor_scalar(
            st_c[:], pb[1][0:1, 0:NT], 1.0 / D, None, ALU.mult))
        mkwait("scalar", sV, vr)
        mkop("scalar", lambda e: e.activation(st_c[:], st_c[:], ACT.Ln,
                                              bias=eps_ln_c[0:1, 0:1]))
        mkop("scalar", lambda e: e.drain())
        qk = mkinc("scalar", sC, 1, lambda e: e.activation(
            st_c[:], st_c[:], ACT.Exp, scale=-0.5))
        mkwait("tensor", sC, qk)
        ln1_qbc[t] = mkinc("tensor", sT, 1, lambda e: e.matmul(
            pb[3][:, 0:NT], ones_row[:], st_c[:], start=True, stop=True))
        mkwait("vector", sT, ln1_qbc[t])

        def l1ap(e):
            last = None
            for c in range(8):
                last = e.tensor_mul(wb.ap()[:, c, :], wb.ap()[:, c, :],
                                    pb[3][:, 0:NT])
            return last
        ln1_ap[t] = mkinc("vector", sV, 1, l1ap)

    def p2_ucast(t):
        mkwait("tensor", sV, ln1_ap[t])
        kwarm()
        mkwait("scalar", sV, ln1_ap[t])
        if t > 0:
            mkwait("scalar", sT, ff_mmk[(t - 1, KC - 1)])
        u_s[t] = mkinc("scalar", sC, 1, lambda e: e.activation(
            UHF, WF[t % 2], ACT.Copy))
        mkwait("vector", sC, u_s[t])
        u_v[t] = mkinc("vector", sV, 1, lambda e: e.tensor_sub(
            ULF, WF[t % 2], UHF))

    def p2_ffn(t):
        for ic in range(KC):
            fpf = ic % 2
            if ic >= 2:
                mkwait("gpsimd", sT, ff_mmk[(t, ic - 2)])
            elif t > 0:
                mkwait("gpsimd", sT, ff_mmk[(t - 1, KC - 2 + ic)])
            mkinc("gpsimd", sQf[fpf], 16, lambda e, ic=ic, fpf=fpf: e.dma_start(
                out=fbuf[fpf][:], in_=ffnT_sc[ic]))
            mkwait("tensor", sQf[fpf])
            if ic == 0:
                mkwait("tensor", sV, u_v[t])
                mkwait("tensor", sC, u_s[t])

            def ffn_mm(e, ic=ic, fpf=fpf):
                last = None
                for oc in range(8):
                    ft = fbuf[fpf].ap()[:, oc * 128:(oc + 1) * 128]
                    bk = pb[oc // 2]
                    o0 = (oc % 2) * NT
                    e.matmul(bk[:, o0:o0 + NT], ft, uhb.ap()[:, ic, :],
                             start=(ic == 0 and oc % 2 == 0), stop=False)
                    last = e.matmul(bk[:, o0:o0 + NT], ft, ulb.ap()[:, ic, :],
                                    start=False,
                                    stop=(ic == KC - 1 and oc % 2 == 1))
                return last
            ff_mmk[(t, ic)] = mkinc("tensor", sT, 1, ffn_mm)

    def p2_v2(t):
        mkwait("vector", sT, ff_mmk[(t, KC - 1)])

        def v2_ops(e):
            last = None
            for oc in range(8):
                last = e.scalar_tensor_tensor(
                    V.ap()[:, oc, :],
                    pb[oc // 2][:, (oc % 2) * NT:(oc % 2 + 1) * NT],
                    scale_col[:, 0:1], W[t % 2].ap()[:, oc, :],
                    ALU.mult, ALU.add)
            return last
        v2_mk[t] = mkinc("vector", sV, 1, v2_ops)

    def wv_ln2_a(t):
        mkwait("tensor", sV, v2_mk[t])

        def l2sum(e):
            last = None
            for c in range(8):
                last = e.matmul(pb[6][0:1, 0:NT], ones512[:, 0:1],
                                V.ap()[:, c, :], start=(c == 0), stop=(c == 7))
            return last
        ln2_sum[t] = mkinc("tensor", sT, 1, l2sum)
        mkwait("vector", sT, ln2_sum[t])
        ln2_mrow[t] = mkinc("vector", sV, 1, lambda e: e.tensor_scalar(
            st_b[:], pb[6][0:1, 0:NT], 1.0 / D, None, ALU.mult))

    def wv_ln2_b(t):
        mkwait("tensor", sV, ln2_mrow[t])
        ln2_mbc[t] = mkinc("tensor", sT, 1, lambda e: e.matmul(
            pb[7][:, 0:NT], ones_row[:], st_b[:], start=True, stop=True))
        mkwait("vector", sT, ln2_mbc[t])
        mkwait("vector", sC, z_s.get(t + 1, 0))

        def l2ctr(e):
            last = None
            for c in range(8):
                last = e.tensor_sub(M.ap()[:, c, :], V.ap()[:, c, :],
                                    pb[7][:, 0:NT])
            return last
        ln2_ctr[t] = mkinc("vector", sV, 1, l2ctr)

    def wv_ln2_c(t):
        mkwait("scalar", sV, ln2_ctr[t])
        mkwait("scalar", sT, ln1_ssq[t])
        ln2_sq[t] = mkinc("scalar", sC, 1, lambda e: e.activation(
            LNLF, MF, ACT.Square))
        mkwait("tensor", sC, ln2_sq[t])
        mkwait("tensor", sV, ln2_mrow[t])

        def l2ssq(e):
            last = None
            for c in range(8):
                last = e.matmul(pb[6][0:1, 0:NT], ones_c16[:], lnl.ap()[:, c, :],
                                start=(c == 0), stop=(c == 7))
            return last
        ln2_ssq[t] = mkinc("tensor", sT, 1, l2ssq)
        mkwait("vector", sT, ln2_ssq[t])
        vr = mkinc("vector", sV, 1, lambda e: e.tensor_scalar(
            st_c[:], pb[6][0:1, 0:NT], 1.0 / D, None, ALU.mult))
        mkwait("scalar", sV, vr)
        mkop("scalar", lambda e: e.activation(st_c[:], st_c[:], ACT.Ln,
                                              bias=eps_ln_c[0:1, 0:1]))
        mkop("scalar", lambda e: e.drain())
        ln2_rs[t] = mkinc("scalar", sC, 1, lambda e: e.activation(
            st_c[:], st_c[:], ACT.Exp, scale=-0.5))

    def wv_ln2_d(t):
        mkwait("tensor", sC, ln2_rs[t])
        mkwait("tensor", sV, ln2_ctr[t])
        ln2_qbc[t] = mkinc("tensor", sT, 1, lambda e: e.matmul(
            pb[7][:, 0:NT], ones_row[:], st_c[:], start=True, stop=True))
        mkwait("vector", sT, ln2_qbc[t])

        def l2ap(e):
            last = None
            for c in range(8):
                last = e.tensor_mul(M.ap()[:, c, :], M.ap()[:, c, :],
                                    pb[7][:, 0:NT])
            return last
        ln2_ap[t] = mkinc("vector", sV, 1, l2ap)

    def wv_out(t, g):
        mkwait("tensor", sV, ln2_ap[t] if g == 0 else oev[(t, 0)])

        def out_tp(e, g=g):
            last = None
            for c in range(8):
                bank = pb[6 + c // 4]
                last = e.transpose(bank[:, (c % 4) * 128:(c % 4) * 128 + 128],
                                   M.ap()[:, c, g * 128:(g + 1) * 128],
                                   ident_s[:])
            return last
        otm[(t, g)] = mkinc("tensor", sT, 1, out_tp)
        mkwait("vector", sT, otm[(t, g)])
        mkwait("vector", sOo[g], out_omk.get((t - 1, g), 0))

        def out_ev(e, g=g):
            e.tensor_copy(outst[g][:, 0:512], pb[6][:, 0:512])
            return e.tensor_copy(outst[g][:, 512:1024], pb[7][:, 0:512])
        oev[(t, g)] = mkinc("vector", sV, 1, out_ev)
        mkwait("gpsimd", sV, oev[(t, g)])
        out_omk[(t, g)] = mkinc("gpsimd", sOo[g], 16,
                                lambda e, t=t, g=g: e.dma_start(
                                    out=y[t * NT + g * 128:t * NT + (g + 1) * 128, :],
                                    in_=outst[g][:]))

    def halo_src(t):
        return xnext if t == NTILE - 1 else xm[(t + 1) * NT:(t + 1) * NT + 1, :]

    def emit_dn(ti, mcd):
        dpfd = mcd % 3
        mkwait("tensor", sQd[dpfd])
        mkwait("tensor", sC, hhc_mk[(ti, mcd)])
        mkwait("tensor", sV, hls_mk[(ti, mcd)])
        if mcd == 0:
            mkwait("tensor", sV, v2_mk.get(ti - 1, 0))

        def dn_mm(e, mcd=mcd, dpfd=dpfd):
            last = None
            for dc in range(8):
                wh_ = dbuf[dpfd].ap()[:, 0, dc * 128:(dc + 1) * 128]
                wl_ = dbuf[dpfd].ap()[:, 1, dc * 128:(dc + 1) * 128]
                hch = hh.ap()[:, mcd % 8, :]
                hcl = hl.ap()[:, mcd % 8, :]
                bk = pb[dc // 2]
                o0 = (dc % 2) * NT
                e.matmul(bk[:, o0:o0 + NT], wh_, hch,
                         start=(mcd == 0 and dc % 2 == 0), stop=False)
                e.matmul(bk[:, o0:o0 + NT], wh_, hcl, start=False, stop=False)
                last = e.matmul(bk[:, o0:o0 + NT], wl_, hch, start=False,
                                stop=(mcd == MC - 1 and dc % 2 == 1))
            return last
        dn_mmk[(ti, mcd)] = mkinc("tensor", sT, 1, dn_mm)

    def dbg_tap3(name, src_ap, width):
        if not _DBG:
            return
        mkwait("sync", sV, sV.n)
        mkwait("sync", sC, sC.n)
        mkwait("sync", sT, sT.n)
        mkinc("sync", sOo[0], 16, lambda e, name=name, src_ap=src_ap,
              width=width: e.dma_start(out=dbg[name][:, 0:width], in_=src_ap))
        mkwait("vector", sOo[0], sOo[0].n)
        mkwait("scalar", sOo[0], sOo[0].n)
        mkwait("tensor", sOo[0], sOo[0].n)

    mkwait("sync", sG, sG.n)
    mkwait("vector", sG, sG.n)
    mkwait("scalar", sG, sG.n)
    mkwait("tensor", sV, sV.n)
    mkwait("scalar", sT, sT.n)
    mkwait("vector", sC, sC.n)
    mkwait("sync", sC, sC.n)
    mkwait("sync", sV, sV.n)
    mkwait("sync", sT, sT.n)

    def issue_stripe(ti, si):
        pbf = si % 2
        m0 = si * MSTRIPE
        mkinc("sync", sQg[pbf], 16, lambda e, m0=m0, pbf=pbf: e.dma_start(
            out=gbuf[pbf][:],
            in_=gate_sc.rearrange("k h p m -> p k h m")[:, :, :, m0:m0 + MSTRIPE]))
        mkinc("sync", sQu[pbf], 16, lambda e, m0=m0, pbf=pbf: e.dma_start(
            out=ubuf[pbf][:],
            in_=up_sc.rearrange("k h p m -> p k h m")[:, :, :, m0:m0 + MSTRIPE]))

    w_load(0, 0)
    w_load(0, 1)
    issue_stripe(0, 0)
    issue_stripe(0, 1)
    w_halo_S(xprev)
    w_halo_xr()
    lh_ev = w_halo_tp_ev(0, 0)
    w_rms(0, 0)
    w_xr(0, 0)
    w_rms(0, 1)
    w_xr(0, 1)
    w_tpA(0, 0, lh_ev)
    w_evA(0, 0)
    w_tpX(0, 0, evA[(0, 0)])
    w_evX(0, 0)
    w_tpA(0, 1, evX[(0, 0)])
    w_evA(0, 1)
    w_tpX(0, 1, evA[(0, 1)])
    w_evX(0, 1)
    w_halo_S(halo_src(0))
    w_halo_xr()
    rhalo_ev[0] = w_halo_tp_ev(257, evX[(0, 1)])
    w_conv(0)
    dbg_tap3("d_carry", gcarry[:], 8)
    mkwait("tensor", sV, sV.n)
    kwarm()
    w_scan(0)
    dbg_tap3("d_conv", VF, 2048)
    mkwait("tensor", sV, sV.n)
    kwarm()
    w_mid(0)
    mkwait("tensor", sV, sV.n)
    kwarm()
    w_rms2(0)
    w_mbz(0, 0)
    w_mbz(0, 1)
    p2_zcast(0)
    mkwait("tensor", sC, z_s[0])
    kwarm()

    def dbg_tap2(name, src_ap, width):
        if not _DBG:
            return
        mkwait("sync", sV, sV.n)
        mkwait("sync", sC, sC.n)
        mkwait("sync", sT, sT.n)
        mkinc("sync", sOo[0], 16, lambda e, name=name, src_ap=src_ap,
              width=width: e.dma_start(out=dbg[name][:, 0:width], in_=src_ap))
        mkwait("vector", sOo[0], sOo[0].n)
        mkwait("scalar", sOo[0], sOo[0].n)
        mkwait("tensor", sOo[0], sOo[0].n)

    dbg_tap2("d_A", A.ap().rearrange("p c t -> p (c t)"), 8 * 258)
    dbg_tap2("d_h", VF, 2048)
    dbg_tap2("d_w1", WF[0], 2048)
    dbg_tap2("d_z", MF, 2048)


    for ti in range(NTILE):
        weave = {}

        def addw(si, fn):
            weave.setdefault(si, []).append(fn)
        if ti >= 1:
            tp_ = ti - 1
            addw(1, lambda tp_=tp_: wv_ln2_a(tp_))
            addw(2, lambda tp_=tp_: wv_ln2_b(tp_))
            addw(3, lambda tp_=tp_: wv_ln2_c(tp_))
            addw(4, lambda tp_=tp_: wv_ln2_d(tp_))
            addw(5, lambda tp_=tp_: wv_out(tp_, 0))
            addw(6, lambda tp_=tp_: wv_out(tp_, 1))
        if ti + 1 < NTILE:
            tf = ti + 1
            addw(3, lambda tf=tf: w_load(tf, 0))
            addw(4, lambda tf=tf: (w_rms(tf, 0), w_lhcopy(tf)))
            addw(5, lambda tf=tf: (w_xr(tf, 0), w_load(tf, 1)))
            addw(6, lambda tf=tf: w_rms(tf, 1))
            addw(7, lambda tf=tf: (w_xr(tf, 1),
                                   w_tpA(tf, 0, oev.get((tf - 2, 1), 0)),
                                   w_evA(tf, 0)))
            addw(8, lambda tf=tf: (w_tpX(tf, 0, evA[(tf, 0)]), w_evX(tf, 0),
                                   w_halo_S(halo_src(tf))))
            addw(9, lambda tf=tf: (w_tpA(tf, 1, evX[(tf, 0)]), w_evA(tf, 1),
                                   w_halo_xr()))
            addw(10, lambda tf=tf: (w_tpX(tf, 1, evA[(tf, 1)]), w_evX(tf, 1)))
            addw(11, lambda tf=tf: (rhalo_ev.__setitem__(
                                        tf, w_halo_tp_ev(257, evX[(tf, 1)])),
                                    w_conv(tf)))
            addw(12, lambda tf=tf: (w_scan(tf), w_mid(tf)))
            addw(13, lambda tf=tf: w_rms2(tf))
            addw(14, lambda tf=tf: (w_mbz(tf, 0), w_mbz(tf, 1)))

        for si in range(NSTRIPE):
            for fn in weave.get(si, []):
                fn()
            pbf = si % 2
            if not (ti == 0 and si < 2):
                if si >= 2:
                    mkwait("sync", sT, gu_mmk[(ti, si - 2)])
                elif ti > 0:
                    mkwait("sync", sT, gu_mmk[(ti - 1, NSTRIPE - 2 + si)])
                issue_stripe(ti, si)
            if si == 0:
                mkwait("tensor", sC, z_s[ti])
                mkwait("tensor", sV, z_v[ti])
            mkwait("tensor", sQg[pbf])
            mkwait("tensor", sQu[pbf])
            for sub in range(MSTRIPE // 128):
                mc = si * (MSTRIPE // 128) + sub
                bank = pb[4 + (mc % 2)]
                dpf = mc % 3
                if mc >= 3:
                    mkwait("gpsimd", sT, dn_mmk[(ti, mc - 3)])
                elif ti > 0:
                    prev_mc = {0: 30, 1: 31, 2: 29}[mc]
                    mkwait("gpsimd", sT, dn_mmk[(ti - 1, prev_mc)])
                else:
                    mkwait("gpsimd", sG, sG.n)
                mkinc("gpsimd", sQd[dpf], 16, lambda e, mc=mc, dpf=dpf: e.dma_start(
                    out=dbuf[dpf][:],
                    in_=down_sc.rearrange("m h p d -> m p h d")[mc]))
                if mc >= 2:
                    mkwait("tensor", sC, silu_mk[(ti, mc - 2)])
                    mkwait("tensor", sV, htmp_mk[(ti, mc - 2)])
                elif ti > 0:
                    mkwait("tensor", sC, silu_mk[(ti - 1, MC - 2 + mc)])
                    mkwait("tensor", sV, htmp_mk[(ti - 1, MC - 2 + mc)])

                def gu_mm(e, pbf=pbf, sub=sub, bank=bank):
                    o0 = sub * 128
                    last = None
                    for k in range(KC):
                        gh = gbuf[pbf].ap()[:, k, 0, o0:o0 + 128]
                        gl = gbuf[pbf].ap()[:, k, 1, o0:o0 + 128]
                        uh_ = ubuf[pbf].ap()[:, k, 0, o0:o0 + 128]
                        ul_ = ubuf[pbf].ap()[:, k, 1, o0:o0 + 128]
                        zch = zh.ap()[:, k, :]
                        zcl = zl.ap()[:, k, :]
                        e.matmul(bank[:, 0:NT], gh, zch, start=(k == 0), stop=False)
                        e.matmul(bank[:, 0:NT], gh, zcl, start=False, stop=False)
                        e.matmul(bank[:, 0:NT], gl, zch, start=False, stop=False)
                        e.matmul(bank[:, NT:2 * NT], uh_, zch, start=False,
                                 stop=False)
                        e.matmul(bank[:, NT:2 * NT], uh_, zcl, start=False,
                                 stop=False)
                        last = e.matmul(bank[:, NT:2 * NT], ul_, zch, start=False,
                                        stop=(k == KC - 1))
                    return last
                gu_mmk[(ti, si)] = mkinc("tensor", sT, 1, gu_mm)
                mkwait("scalar", sT, gu_mmk[(ti, si)])
                if mc >= 2:
                    mkwait("scalar", sV, htmp_mk[(ti, mc - 2)])
                elif ti > 0:
                    mkwait("scalar", sV, htmp_mk[(ti - 1, MC - 2 + mc)])
                smk = mkinc("scalar", sC, 1, lambda e, mc=mc, bank=bank:
                            e.activation(sgb[mc % 2][:], bank[:, 0:NT], ACT.Silu,
                                         bias=bgcol[:, mc:mc + 1]))
                silu_mk[(ti, mc)] = smk
                mkwait("vector", sC, smk)
                htmp_mk[(ti, mc)] = mkinc(
                    "vector", sV, 1, lambda e, mc=mc, bank=bank:
                    e.scalar_tensor_tensor(htmp[mc % 2][:], bank[:, NT:2 * NT],
                                           bucol[:, mc:mc + 1], sgb[mc % 2][:],
                                           ALU.add, ALU.mult))
                ring_prev = (dn_mmk[(ti, mc - 8)] if mc >= 8 else
                             (dn_mmk[(ti - 1, MC - 8 + mc)] if ti > 0 else 0))
                mkwait("scalar", sV, htmp_mk[(ti, mc)])
                mkwait("scalar", sT, ring_prev)
                hhc_mk[(ti, mc)] = mkinc(
                    "scalar", sC, 1, lambda e, mc=mc: e.activation(
                        hh.ap()[:, mc % 8, :], htmp[mc % 2][:], ACT.Copy))
                mkwait("vector", sC, hhc_mk[(ti, mc)])
                mkwait("vector", sT, ring_prev)
                hls_mk[(ti, mc)] = mkinc(
                    "vector", sV, 1, lambda e, mc=mc: e.tensor_sub(
                        hl.ap()[:, mc % 8, :], htmp[mc % 2][:],
                        hh.ap()[:, mc % 8, :]))
                if mc >= 1:
                    emit_dn(ti, mc - 1)
        emit_dn(ti, MC - 1)

        if ti + 1 < NTILE:
            p2_zcast(ti + 1)
        p2_v1(ti)
        p2_ln1(ti)
        p2_ucast(ti)
        p2_ffn(ti)
        p2_v2(ti)

    wv_ln2_a(NTILE - 1)
    wv_ln2_b(NTILE - 1)
    wv_ln2_c(NTILE - 1)
    wv_ln2_d(NTILE - 1)
    wv_out(NTILE - 1, 0)
    wv_out(NTILE - 1, 1)

    mkwait("sync", sOo[0], sOo[0].n)
    mkwait("sync", sOo[1], sOo[1].n)

    by_eng = {"sync": [], "scalar": [], "vector": [], "tensor": [], "gpsimd": []}
    for name, fn in sched:
        by_eng[name].append(fn)
    block.sync(lambda e: [fn(e) for fn in by_eng["sync"]])
    block.scalar(lambda e: [fn(e) for fn in by_eng["scalar"]])
    block.vector(lambda e: [fn(e) for fn in by_eng["vector"]])
    block.tensor(lambda e: [fn(e) for fn in by_eng["tensor"]])
    block.gpsimd(lambda e: [fn(e) for fn in by_eng["gpsimd"]])
    ctx.close()

    _simulate(simlog)
    return nc


def _simulate(simlog):
    from collections import defaultdict, deque
    qs = defaultdict(deque)
    for eng, kind, semid, val in simlog:
        qs[eng].append((kind, semid, val))
    cur = defaultdict(int)
    progress = True
    while progress:
        progress = False
        for eng in list(qs):
            q = qs[eng]
            while q:
                kind, semid, val = q[0]
                if kind == "i":
                    cur[semid] = max(cur[semid], val)
                elif cur[semid] < val:
                    break
                q.popleft()
                progress = True
    stuck = {e: len(q) for e, q in qs.items() if q}
    assert not stuck, f"schedule deadlock, remaining: {stuck}"


_NC_CACHE = None


def kernel(L0_attn_x, L0_attn_norm_w, L0_attn_conv_w, L0_attn_conv_b, L0_attn_a,
           L0_attn_b, L0_attn_mlp_norm_w, L0_attn_w_gate, L0_attn_b_gate,
           L0_attn_w_up, L0_attn_b_up, L0_attn_w_down, L0_attn_b_down,
           L0_ffn_weight):
    global _NC_CACHE
    x = np.ascontiguousarray(np.asarray(L0_attn_x, dtype=np.float32).reshape(T_FULL, D))
    cvec = np.zeros((D, 8), np.float32)
    cvec[:, 0] = np.asarray(L0_attn_norm_w, np.float32)
    cw = np.asarray(L0_attn_conv_w, np.float32).reshape(D, 3)
    cvec[:, 1], cvec[:, 2], cvec[:, 3] = cw[:, 0], cw[:, 1], cw[:, 2]
    cvec[:, 4] = np.asarray(L0_attn_conv_b, np.float32)
    cvec[:, 5] = np.asarray(L0_attn_a, np.float32)
    cvec[:, 6] = np.asarray(L0_attn_b, np.float32)
    cvec[:, 7] = np.asarray(L0_attn_b_down, np.float32)
    bgu = np.ascontiguousarray(np.stack(
        [np.asarray(L0_attn_b_gate, np.float32),
         np.asarray(L0_attn_b_up, np.float32)], axis=1))
    mnw = np.ascontiguousarray(np.asarray(L0_attn_mlp_norm_w, np.float32)[None, :])
    ident = np.eye(128, dtype=np.float32)
    wg = np.ascontiguousarray(np.asarray(L0_attn_w_gate, np.float32))
    wu = np.ascontiguousarray(np.asarray(L0_attn_w_up, np.float32))
    wd = np.ascontiguousarray(np.asarray(L0_attn_w_down, np.float32))
    wf = np.ascontiguousarray(np.asarray(L0_ffn_weight, np.float32))

    if _NC_CACHE is None:
        _NC_CACHE = build_program()
    nc = _NC_CACHE

    in_maps = []
    for c in range(NCORE):
        s0 = c * NTOK
        e0 = s0 + NTOK
        xpad = np.zeros((PRE, D), np.float32)
        if s0 > 0:
            xpad[PRE - s0:] = x[:s0]
        in_maps.append({
            "xm": np.ascontiguousarray(x[s0:e0]),
            "xp": xpad,
            "xprev": np.ascontiguousarray(x[s0 - 1:s0]) if s0 > 0
                     else np.zeros((1, D), np.float32),
            "xnext": np.ascontiguousarray(x[e0:e0 + 1]) if e0 < T_FULL
                     else np.zeros((1, D), np.float32),
            "w_gate": wg, "w_up": wu, "w_down": wd, "w_ffn": wf,
            "cvec": cvec, "bgu": bgu, "mnw": mnw, "ident": ident,
        })
    res = run_bass_kernel_spmd(nc, in_maps, core_ids=list(range(NCORE)))
    out = np.concatenate([res.results[c]["y"] for c in range(NCORE)], axis=0)
    return out.reshape(1, T_FULL, D)
